# version 16
# baseline (speedup 1.0000x reference)
import numpy as np
import ml_dtypes
import concourse.bass as bass
import concourse.mybir as mybir
from concourse.bass_utils import run_bass_kernel_spmd
from contextlib import ExitStack

F32 = mybir.dt.float32
BF16 = mybir.dt.bfloat16
AF = mybir.ActivationFunctionType
ALU = mybir.AluOpType

D = 1024
T = 2048
CH = 8
TT = 512
NT = T // TT
I_FF = 4096
NE = 4
NIC = I_FF // 128

V_TMK, V_TMV, V_TMR, V_W, V_U, V_GAM, V_BET, V_BO = range(8)
R_MU, R_RSTD, R_LNT, R_M, R_LROW = 0, 512, 1024, 512, 0


class Sem:
    def __init__(self, ctx, nc, name):
        self.h = ctx.enter_context(nc.semaphore(name))
        self.n = 0

    def bump(self, amt):
        self.n += amt
        return (self, self.n)


def build():
    nc = bass.Bass("TRN2", target_bir_lowering=False, dynamic_dma_scratch_size=4096,
                   detect_race_conditions=False)

    x_h = nc.dram_tensor("x", [D, T], F32, kind="ExternalInput").ap()
    wk_h = nc.dram_tensor("wkT", [D, D], F32, kind="ExternalInput").ap()
    wv_h = nc.dram_tensor("wvT", [D, D], F32, kind="ExternalInput").ap()
    wr_h = nc.dram_tensor("wrT", [D, D], F32, kind="ExternalInput").ap()
    wo_h = nc.dram_tensor("woT", [D, D], F32, kind="ExternalInput").ap()
    w1_h = nc.dram_tensor("w1", [NIC, 128, NE, CH, 128], F32, kind="ExternalInput").ap()
    wo2_h = nc.dram_tensor("wo2", [CH, 128, NIC, 128], F32, kind="ExternalInput").ap()
    b1_h = nc.dram_tensor("b1", [NE, I_FF], F32, kind="ExternalInput").ap()
    wrt_h = nc.dram_tensor("wrt", [128, CH * NE], F32, kind="ExternalInput").ap()
    vecs_h = nc.dram_tensor("vecs_in", [128, 64], F32, kind="ExternalInput").ap()
    y_h = nc.dram_tensor("y", [D, T], F32, kind="ExternalOutput").ap()

    ck_h = nc.dram_tensor("ck", [D, D], BF16, kind="Internal").ap()
    cv_h = nc.dram_tensor("cv", [D, D], BF16, kind="Internal").ap()
    cr_h = nc.dram_tensor("cr", [D, D], BF16, kind="Internal").ap()
    co_h = nc.dram_tensor("co", [D, D], BF16, kind="Internal").ap()
    cw1_h = nc.dram_tensor("cw1", [NIC, 128, NE, CH, 128], BF16, kind="Internal").ap()
    cwo2_h = nc.dram_tensor("cwo2", [CH, 128, NIC, 128], BF16, kind="Internal").ap()

    with ExitStack() as ctx:
        def sb(name, shp, dt=F32):
            return ctx.enter_context(nc.sbuf_tensor(name, shp, dt))

        xb = sb("xb", [128, CH * TT])
        xn = sb("xn", [128, CH * TT])
        kb = sb("kb", [128, CH * TT])
        vb = sb("vb", [128, CH * TT])
        rb = sb("rb", [128, CH * TT])
        Ab = sb("Ab", [128, CH * TT])
        Bb = sb("Bb", [128, CH * TT])
        Pb = sb("Pb", [128, CH * (TT + 1)])
        Qb = sb("Qb", [128, CH * (TT + 1)])
        sbf1 = sb("sbf1", [128, CH * TT], BF16)
        sbf2 = sb("sbf2", [128, CH * TT], BF16)
        sbf3 = sb("sbf3", [128, CH * TT], BF16)
        sbf5 = sb("sbf5", [128, CH * TT], BF16)
        ws = [sb("ws0", [128, 4096], BF16), sb("ws1", [128, 4096], BF16)]
        b1b = sb("b1b", [NE, I_FF], BF16)
        wrtb = sb("wrtb", [128, CH * NE], BF16)
        vecs = sb("vecs", [128, 64], F32)
        pc = sb("pc", [128, CH], F32)
        qc = sb("qc", [128, CH], F32)
        xnl = sb("xnl", [128, CH], F32)
        rowf = sb("rowf", [1, 2048], F32)
        masks4 = sb("masks4", [NE, TT], BF16)
        onesK = sb("onesK", [128, 1], BF16)
        onesrB = sb("onesrB", [1, 128], BF16)
        onesF = sb("onesF", [1, 128], F32)
        ones4F = sb("ones4F", [1, NE], F32)
        epsb = sb("epsb", [1, 2], F32)

        pbank = [ctx.enter_context(nc.psum_tensor(f"pb{i}", [128, TT], F32)) for i in range(4)]
        pbc0 = ctx.enter_context(nc.psum_tensor("pbc0", [128, TT], F32))
        pbc1 = ctx.enter_context(nc.psum_tensor("pbc1", [128, TT], F32))
        prow0 = ctx.enter_context(nc.psum_tensor("prow0", [NE, TT], F32))
        prow1 = ctx.enter_context(nc.psum_tensor("prow1", [NE, TT], F32))

        sP = Sem(ctx, nc, "sP")
        sG = [Sem(ctx, nc, "sG0"), Sem(ctx, nc, "sG1")]
        sU = [Sem(ctx, nc, "sU0"), Sem(ctx, nc, "sU1")]
        sS = [Sem(ctx, nc, "sS0"), Sem(ctx, nc, "sS1")]
        sV = Sem(ctx, nc, "sV")
        sX = Sem(ctx, nc, "sX")
        sR = Sem(ctx, nc, "sR")
        sW = Sem(ctx, nc, "sW")
        sMM = Sem(ctx, nc, "sMM")
        sDV = Sem(ctx, nc, "sDV")
        sAC = Sem(ctx, nc, "sAC")
        sPL = Sem(ctx, nc, "sPL")
        sST = Sem(ctx, nc, "sST")

        block = ctx.enter_context(nc.Block())
        prog = {k: [] for k in ("sync", "vector", "scalar", "tensor", "gpsimd")}

        def emit(ename, fn, waits=(), inc=None, amt=1):
            mark = inc.bump(amt) if inc is not None else None
            prog[ename].append((tuple(w for w in waits if w is not None), fn,
                                inc.h if inc else None, amt))
            return mark

        def c3(t):
            return t.ap().rearrange("p (c t) -> p c t", c=CH)

        x3 = c3(xb); xn3 = c3(xn); k3 = c3(kb); v3 = c3(vb); r3 = c3(rb)
        A3 = c3(Ab); B3 = c3(Bb)
        P3 = Pb.ap().rearrange("p (c t) -> p c t", c=CH)
        Q3 = Qb.ap().rearrange("p (c t) -> p c t", c=CH)
        Ppre = P3[:, :, 0:TT]; Pcur = P3[:, :, 1:TT + 1]
        Qpre = Q3[:, :, 0:TT]
        sb13 = c3(sbf1); sb23 = c3(sbf2); sb33 = c3(sbf3)
        kbf = kb.ap().bitcast(BF16)
        Abf = Ab.ap().bitcast(BF16)
        Bbf = Bb.ap().bitcast(BF16)
        xbbf = xb.ap().bitcast(BF16)

        rbbf = rb.ap().bitcast(BF16)
        mrow = sbf5.ap()[0:1, 2048:4096]
        logits4 = Pb.ap()[0:4, 0:TT]
        rbf3 = rbbf[:, 0:CH * TT].rearrange("p (c t) -> p c t", c=CH)

        def xne_ap(e):
            return [sbf3.ap(), sbf2.ap(), sbf5.ap(),
                    rbbf[:, CH * TT:2 * CH * TT]][e]

        def hview(ic):
            if ic < 16:
                return Abf[:, ic * TT:(ic + 1) * TT]
            return xbbf[:, (ic - 16) * TT:(ic - 15) * TT]

        def vcol(v, c):
            return vecs[:, v * CH + c:v * CH + c + 1]

        m_prep = {}
        prev_prep = [None]

        def prep(name, fn):
            mk = emit("gpsimd", fn, waits=[prev_prep[0]], inc=sP, amt=16)
            prev_prep[0] = mk
            m_prep[name] = mk
            return mk
        prep("k", lambda g: g.dma_start(out=ck_h, in_=wk_h))
        prep("v", lambda g: g.dma_start(out=cv_h, in_=wv_h))
        prep("r", lambda g: g.dma_start(out=cr_h, in_=wr_h))
        prep("wrt", lambda g: g.dma_start(out=wrtb[:], in_=wrt_h))
        prep("b1", lambda g: g.dma_start(out=b1b[:], in_=b1_h))
        w1q = list(range(16))
        for j in w1q[:2]:
            prep(("w1q", j), lambda g, j_=j: g.dma_start(out=cw1_h[2 * j_:2 * (j_ + 1)],
                                                         in_=w1_h[2 * j_:2 * (j_ + 1)]))
        prep("o", lambda g: g.dma_start(out=co_h, in_=wo_h))
        for j in w1q[2:]:
            prep(("w1q", j), lambda g, j_=j: g.dma_start(out=cw1_h[2 * j_:2 * (j_ + 1)],
                                                         in_=w1_h[2 * j_:2 * (j_ + 1)]))
        prep(("wo2h", 0), lambda g: g.dma_start(out=cwo2_h[0:4], in_=wo2_h[0:4]))
        prep(("wo2h", 1), lambda g: g.dma_start(out=cwo2_h[4:8], in_=wo2_h[4:8]))
        m_prep_all = m_prep["b1"]
        emit("sync", lambda s: s.dma_start(out=vecs[:], in_=vecs_h), inc=sV, amt=16)
        m_vecs = (sV, sV.n)

        def _init(v):
            v.memset(pc[:], -1e30)
            v.memset(qc[:], 0.0)
            v.memset(xnl[:], 0.0)
            v.memset(onesK[:], 1.0)
            v.memset(onesrB[:], 1.0)
            v.memset(onesF[:], 1.0)
            v.memset(ones4F[:], 1.0)
            v.memset(epsb[0:1, 0:1], 1e-5)
            return v.memset(epsb[0:1, 1:2], 1e-12)
        emit("vector", _init, inc=sDV)
        m_init = (sDV, sDV.n)

        slot_busy = [None, None]
        unit_loaded = {}
        unit_ctr = [0]
        bank_free = [None, None, None, None]

        def load_unit(uid, src_ap, prep_mark, iu=None):
            n = unit_ctr[0]; unit_ctr[0] += 1
            slot = n % 2
            mk = emit("sync", lambda s, sl=slot, sp=src_ap: s.dma_start(out=ws[sl][:], in_=sp),
                      waits=[prep_mark, slot_busy[slot]], inc=sU[slot], amt=16)
            unit_loaded[uid] = (mk, slot)
            return slot

        ckr = {"k": ck_h.rearrange("(dc dp) e -> dp dc e", dp=128),
               "v": cv_h.rearrange("(dc dp) e -> dp dc e", dp=128),
               "r": cr_h.rearrange("(dc dp) e -> dp dc e", dp=128),
               "o": co_h.rearrange("(dc dp) e -> dp dc e", dp=128)}

        m_prev_out = None
        m_prev_store = None
        m_prev_t3 = None
        mt = {}

        for ti in range(NT):
            EPOOL = "vector" if ti == 0 else "gpsimd"
            sEP = sDV if EPOOL == "vector" else sPL
            t0 = ti * TT
            M = {}

            xsrc = x_h.rearrange("(c p) t -> p c t", p=128)[:, :, t0:t0 + TT]
            xld_h = []
            for hw in range(2):
                xld_h.append(emit("sync",
                             lambda s, sp=xsrc, a=hw * 4, b=hw * 4 + 4: s.dma_start(out=x3[:, a:b], in_=sp[:, a:b]),
                             waits=[mt.get(f"out_w{hw}")], inc=sX, amt=16))
            M["xld"] = xld_h[-1]

            xbf_h = []
            xsq_h = []
            for hw in range(2):
                a, bnd = hw * 4, hw * 4 + 4
                xbf_h.append(emit("vector",
                             lambda v, a_=a, b_=bnd: v.tensor_copy(sb13[:, a_:b_].rearrange("p c t -> p (c t)"), x3[:, a_:b_].rearrange("p c t -> p (c t)")),
                             waits=[xld_h[hw], mt.get("ln2_sum1"), m_init], inc=sDV))
                xsq_h.append(emit("scalar",
                             lambda s, a_=a, b_=bnd: s.activation(sb23[:, a_:b_], x3[:, a_:b_], AF.Square),
                             waits=[xld_h[hw], mt.get("ln2_sum2"), mt.get("w1_last")], inc=sAC))
            M["xbf"] = xbf_h[-1]
            M["xsq"] = xsq_h[-1]

            def stats_group(dst, rhs3, hmarks=None):
                def fn(t):
                    for c in range(CH):
                        if hmarks is not None:
                            s_, v_ = hmarks[c // 4]
                            t.wait_ge(s_.h, v_)
                        ins = t.matmul(dst[0:1, :], onesK[:], rhs3[:, c],
                                       start=(c == 0), stop=(c == CH - 1))
                    return ins
                return fn
            M["sum1"] = emit("tensor", stats_group(prow0, sb13, xbf_h),
                             waits=[mt.get("rowread0")], inc=sMM)
            M["sum2"] = emit("tensor", stats_group(prow1, sb23, xsq_h),
                             waits=[mt.get("rowread1")], inc=sMM)

            def rows_a(v):
                v.tensor_scalar(rowf[:, R_MU:R_MU + TT], prow0[0:1, :], 1.0 / D, None, ALU.mult)
                return v.tensor_scalar(rowf[:, R_RSTD:R_RSTD + TT], prow1[0:1, :], 1.0 / D, None, ALU.mult)

            def rows_b(v):
                v.tensor_tensor(rowf[:, R_LNT:R_LNT + TT], rowf[:, R_MU:R_MU + TT],
                                rowf[:, R_MU:R_MU + TT], ALU.mult)
                return v.tensor_tensor(rowf[:, R_RSTD:R_RSTD + TT], rowf[:, R_RSTD:R_RSTD + TT],
                                       rowf[:, R_LNT:R_LNT + TT], ALU.subtract)
            _ra1 = emit("vector", rows_a,
                        waits=[M["sum1"], M["sum2"], mt.get("rowf_guard")], inc=sDV)
            M["var1"] = emit("vector", rows_b, waits=[_ra1], inc=sDV)

            def rstd_a(eps):
                def fn(s):
                    s.activation(rowf[:, R_LNT:R_LNT + TT], rowf[:, R_RSTD:R_RSTD + TT],
                                 AF.Ln, bias=eps, scale=1.0)
                    return s.activation(rowf[:, R_RSTD:R_RSTD + TT], rowf[:, R_LNT:R_LNT + TT],
                                        AF.Exp, bias=0.0, scale=-0.5)
                return fn
            M["rstd1"] = emit("scalar", rstd_a(epsb[0:1, 0:1]), waits=[M["var1"], m_init], inc=sAC)

            M["bc_mu1"] = emit("tensor",
                               lambda t: t.matmul(pbc0[:], onesF[:], rowf[0:1, R_MU:R_MU + TT],
                                                  start=True, stop=True),
                               waits=[M["rstd1"], mt.get("pbc_guard")], inc=sMM)
            M["bc_rs1"] = emit("tensor",
                               lambda t: t.matmul(pbc1[:], onesF[:], rowf[0:1, R_RSTD:R_RSTD + TT],
                                                  start=True, stop=True),
                               waits=[M["rstd1"], mt.get("pbc_guard")], inc=sMM)

            def apply_ln1(v):
                for c in range(CH):
                    v.tensor_tensor(xn3[:, c], x3[:, c], pbc0[:], ALU.subtract)
                for c in range(CH):
                    ins = v.tensor_tensor(xn3[:, c], xn3[:, c], pbc1[:], ALU.mult)
                return ins
            M["xn"] = emit("vector", apply_ln1,
                           waits=[M["bc_mu1"], M["bc_rs1"], mt.get("ffn_res")], inc=sDV)
            M["xnbf"] = emit("vector", lambda v: v.tensor_copy(sbf2[:], xn[:]),
                             waits=[M["xsq"]], inc=sDV)

            def router(t):
                for c in range(CH):
                    ins = t.matmul(prow0[0:NE, :], wrtb[:, c * NE:(c + 1) * NE], sb23[:, c],
                                   start=(c == 0), stop=(c == CH - 1))
                return ins
            M["router"] = emit("tensor", router,
                               waits=[M["xnbf"], m_prep_all, M["var1"]], inc=sMM)

            def diffs(v):
                v.tensor_tensor(A3[:, :, 1:], xn3[:, :, 1:], xn3[:, :, 0:TT - 1], ALU.subtract)
                return v.tensor_tensor(A3[:, :, 0], xn3[:, :, 0], xnl.ap(), ALU.subtract)
            M["diff"] = emit("vector", diffs, waits=[M["xn"], mt.get("wo2_last")], inc=sDV)

            def mk_mix(dst3, vidx):
                def fn(v):
                    for c in range(CH):
                        v.scalar_tensor_tensor(dst3[:, c, 1:], A3[:, c, 1:], vcol(vidx, c),
                                               xn3[:, c, 0:TT - 1], ALU.mult, ALU.add)
                        ins = v.scalar_tensor_tensor(dst3[:, c, 0:1], A3[:, c, 0:1], vcol(vidx, c),
                                                     xnl[:, c:c + 1], ALU.mult, ALU.add)
                    return ins
                return fn
            M["mixk"] = emit("vector", mk_mix(sb33, V_TMK),
                             waits=[M["diff"], m_vecs, mt.get("w1_last")], inc=sDV)
            M["mixv"] = emit("vector", mk_mix(sb13, V_TMV),
                             waits=[M["diff"], m_vecs, M["sum1"]], inc=sDV)

            IU = {"k": 0, "v": 2, "r": 4, "o": 38}

            def proj_groups(nm, rhs3, mixmark, evac):
                load_unit((ti, nm, 0), ckr[nm][:, :, 0:512], m_prep[nm])
                load_unit((ti, nm, 1), ckr[nm][:, :, 512:1024], m_prep[nm])
                for ec in range(CH):
                    mku, slot = unit_loaded[(ti, nm, 0 if ec < 4 else 1)]
                    bank = pbank[ec % 4]
                    wsl = ws[slot].ap().rearrange("p (dc e) -> p dc e", dc=CH)
                    eo = (ec % 4) * 128

                    def fn(t, w=wsl, b=bank, eo_=eo, r3_=rhs3):
                        for dc in range(CH):
                            ins = t.matmul(b[:], w[:, dc, eo_:eo_ + 128], r3_[:, dc],
                                           start=(dc == 0), stop=(dc == CH - 1))
                        return ins
                    gm = emit("tensor", fn, waits=[mku, mixmark, bank_free[ec % 4]], inc=sMM)
                    M[f"{nm}grp{ec}"] = gm
                    ev = evac(gm, ec, bank)
                    M[f"{nm}ev{ec}"] = ev
                    bank_free[ec % 4] = ev
                    if ec == 3:
                        slot_busy[unit_loaded[(ti, nm, 0)][1]] = gm
                    if ec == 7:
                        slot_busy[unit_loaded[(ti, nm, 1)][1]] = gm

            proj_groups("k", sb33, M["mixk"],
                        lambda gm, ec, bank: emit("scalar",
                            lambda s, b=bank, e=ec: s.activation(k3[:, e], b[:], AF.Copy),
                            waits=[gm, mt.get("f")], inc=sAC))
            M["lg4"] = emit("vector", lambda v: v.tensor_copy(logits4, prow0[0:NE, :]),
                            waits=[M["router"], mt.get("masks4"), m_prev_store], inc=sDV)
            proj_groups("v", sb13, M["mixv"],
                        lambda gm, ec, bank: emit("vector",
                            lambda v, b=bank, e=ec: v.tensor_copy(v3[:, e], b[:]),
                            waits=[gm, mt.get("t3")], inc=sDV))

            M["realign"] = emit("sync", lambda s: s.dma_start(
                out=rowf.ap()[0:1, R_LROW:R_LROW + NE * TT].rearrange("p (e t) -> p e t", e=NE),
                in_=logits4), waits=[M["lg4"], mt.get("mrows"), M["bc_mu1"], M["bc_rs1"]],
                inc=sR, amt=16)

            def maxes(v):
                L = rowf[0:1, :]
                v.tensor_tensor(L[:, 0:TT], L[:, 0:TT], L[:, TT:2 * TT], ALU.max)
                v.tensor_tensor(L[:, 2 * TT:3 * TT], L[:, 2 * TT:3 * TT], L[:, 3 * TT:4 * TT], ALU.max)
                return v.tensor_tensor(L[:, R_M:R_M + TT], L[:, 0:TT], L[:, 2 * TT:3 * TT], ALU.max)
            M["mrows"] = emit("vector", maxes,
                              waits=[M["realign"], M["rstd1"], M["bc_mu1"], M["bc_rs1"]],
                              inc=sDV)
            M["mixr"] = emit("vector", mk_mix(sb33, V_TMR),
                             waits=[M["diff"], m_vecs, M["kgrp7"]], inc=sDV)
            proj_groups("r", sb33, M["mixr"],
                        lambda gm, ec, bank: emit("scalar",
                            lambda s, b=bank, e=ec: s.activation(rbf3[:, e], b[:], AF.Tanh,
                                                                 bias=0.0, scale=0.5),
                            waits=[gm, mt.get("t_ln2")], inc=sAC))
            M["xnl"] = emit("vector", lambda v: v.tensor_copy(xnl[:], xn3[:, :, TT - 1]),
                            waits=[M["mixv"], M["mixr"]], inc=sDV)
            M["rfix"] = emit("vector",
                             lambda v: v.tensor_scalar(rbbf[:, 0:CH * TT], rbbf[:, 0:CH * TT],
                                                       0.5, 0.5, ALU.mult, ALU.add),
                             waits=[M["rev7"]], inc=sDV)

            M["m4bc"] = emit("tensor",
                             lambda t: t.matmul(prow1[0:NE, :], ones4F[:], rowf[0:1, R_M:R_M + TT],
                                                start=True, stop=True),
                             waits=[M["mrows"], M["var1"]], inc=sMM)
            M["masks4"] = emit("vector",
                               lambda v: v.tensor_tensor(masks4[:], logits4, prow1[0:NE, :], ALU.is_equal),
                               waits=[M["m4bc"]], inc=sDV)
            M["mrowre"] = emit("sync", lambda s: s.dma_start(
                out=mrow.rearrange("p (e t) -> p e t", e=NE), in_=masks4[:]),
                waits=[M["masks4"], mt.get("maskbc_last"), mt.get("w1_last")], inc=sR, amt=16)

            def bc_mask(e, dst, extra):
                return emit("tensor",
                            lambda t, e_=e, d=dst: t.matmul(d[:], onesrB[:],
                                                            mrow[:, e_ * TT:(e_ + 1) * TT],
                                                            start=True, stop=True),
                            waits=[M["mrowre"], M["xn"]] + extra, inc=sMM)

            def ev_mask(eng, dstbuf, psum, extra):
                def fn(s, d=dstbuf, ps=psum):
                    d3 = d.rearrange("p (c t) -> p c t", c=CH)
                    for c in range(CH):
                        ins = (s.activation(d3[:, c], ps[:], AF.Copy) if eng == "scalar"
                               else s.tensor_copy(d3[:, c], ps[:]))
                    return ins
                return emit(eng, fn, waits=extra, inc=(sAC if eng == "scalar" else sDV))

            M["bc_e0"] = bc_mask(0, pbc0, [])
            M["bc_e1"] = bc_mask(1, pbc1, [])
            M["ev_e0"] = ev_mask("scalar", xne_ap(0), pbc0, [M["bc_e0"], M["rgrp7"], mt.get("w1_last")])
            M["bc_e3"] = bc_mask(3, pbank[0], [bank_free[0]])
            M["ev_e3"] = ev_mask("vector", xne_ap(3), pbank[0], [M["bc_e3"], mt.get("t_ln2")])
            bank_free[0] = M["ev_e3"]
            M["bc_e2"] = bc_mask(2, pbc0, [M["ev_e0"]])
            M["ev_e2"] = ev_mask("scalar", xne_ap(2), pbc0, [M["bc_e2"], mt.get("w1_last")])

            xne_by_e = {}
            for e, eng in ((0, EPOOL), (2, EPOOL), (3, "vector")):
                def fn(v, e_=e):
                    dst = xne_ap(e_).rearrange("p (c t) -> p c t", c=CH)
                    for c in range(CH):
                        ins = v.tensor_tensor(dst[:, c], dst[:, c], sb23[:, c], ALU.mult)
                    return ins
                xne_by_e[e] = emit(eng, fn, waits=[M[f"ev_e{e}"], M["xnbf"]],
                                   inc=(sDV if eng == "vector" else sEP))
            xne_marks = list(xne_by_e.values())

            def mul_e1(v):
                for c in range(CH):
                    ins = v.tensor_tensor(sb23[:, c], sb23[:, c], pbc1[:], ALU.mult)
                return ins
            M["mul_e1"] = emit("vector", mul_e1,
                               waits=[M["bc_e1"], M["router"]] + xne_marks, inc=sDV)
            xne_by_e[1] = M["mul_e1"]

            M["pcin"] = emit("vector",
                             lambda v: (v.tensor_copy(P3[:, :, 0], pc.ap()),
                                        v.tensor_copy(Q3[:, :, 0], qc.ap()))[-1],
                             waits=[m_prev_store, m_prev_out, M["realign"]], inc=sDV)

            def pscan(v):
                for c in range(CH):
                    wb = vcol(V_W, c).broadcast_to([128, TT])
                    ins = v.tensor_tensor_scan(P3[:, c, 1:], wb, k3[:, c], P3[:, c, 0:1],
                                               ALU.add, ALU.max)
                return ins
            M["pscan"] = emit("vector", pscan, waits=[M["pcin"], M["kev7"], M["realign"]], inc=sDV)

            def asub(v):
                for c in range(CH):
                    ins = v.scalar_tensor_tensor(A3[:, c], Ppre[:, c], vcol(V_W, c), Pcur[:, c],
                                                 ALU.add, ALU.subtract)
                return ins
            M["asub"] = emit("vector", asub, waits=[M["pscan"], M["mixr"]], inc=sDV)
            M["aexp"] = emit("scalar", lambda s: s.activation(xb[:], Ab[:], AF.Exp),
                             waits=[M["asub"]], inc=sAC)
            M["bsub"] = emit("vector",
                             lambda v: v.tensor_tensor(B3[:, :, :], k3[:, :, :], Pcur, ALU.subtract),
                             waits=[M["pscan"], mt.get("t3")], inc=sDV)
            M["bexp"] = emit("scalar", lambda s: s.activation(Ab[:], Bb[:], AF.Exp),
                             waits=[M["bsub"]], inc=sAC)
            M["bmul"] = emit("vector", lambda v: v.tensor_mul(Ab[:], Ab[:], vb[:]),
                             waits=[M["bexp"], M["vev7"]], inc=sDV)

            def _sigarg(v):
                for c in range(CH):
                    ins = v.scalar_tensor_tensor(B3[:, c], k3[:, c], vcol(V_U, c), Ppre[:, c],
                                                 ALU.add, ALU.subtract)
                return ins
            M["sigarg"] = emit("vector", _sigarg, waits=[M["bexp"]], inc=sDV)

            def qscan(v):
                for c in range(CH):
                    ins = v.tensor_tensor_scan(Q3[:, c, 1:], x3[:, c], A3[:, c], Q3[:, c, 0:1],
                                               ALU.mult, ALU.add)
                return ins
            M["qscan"] = emit("vector", qscan, waits=[M["aexp"], M["bmul"]], inc=sDV)
            M["tau"] = emit("scalar", lambda s: s.activation(xb[:], Bb[:], AF.Tanh, bias=0.0, scale=0.5),
                            waits=[M["sigarg"], M["qscan"]], inc=sAC)
            M["delta"] = emit(EPOOL, lambda v: v.tensor_tensor(B3[:, :, :], Qpre, v3[:, :, :], ALU.subtract),
                              waits=[M["qscan"], M["tau"]], inc=sEP)
            M["e"] = emit("vector", lambda v: v.tensor_mul(kb[:], xb[:], Bb[:]),
                          waits=[M["tau"], M["delta"]], inc=sDV)
            M["f"] = emit(EPOOL, lambda v: v.tensor_sub(Bb[:], Bb[:], kb[:]),
                          waits=[M["e"]], inc=sEP)
            M["wkv"] = emit("vector",
                            lambda v: v.scalar_tensor_tensor(Bb[:], Bb[:], 0.5, vb[:], ALU.mult, ALU.add),
                            waits=[M["f"]], inc=sDV)
            M["ybf"] = emit("vector", lambda v: v.tensor_mul(sbf1[:], Bb[:], rbbf[:, 0:CH * TT]),
                            waits=[M["wkv"], M["rfix"], M["sum1"]], inc=sDV)
            M["carry"] = emit("vector",
                              lambda v: (v.tensor_copy(pc[:], P3[:, :, TT]),
                                         v.tensor_copy(qc[:], Q3[:, :, TT]))[-1],
                              waits=[M["qscan"]], inc=sDV)

            h_ev = []
            w1_last = None
            for ic in range(NIC):
                load_unit((ti, "w1", ic), cw1_h[ic].rearrange("p e dc i -> p (e dc i)"),
                          m_prep[("w1q", ic // 2)])
                mku, slot = unit_loaded[(ti, "w1", ic)]
                bank = pbank[ic % 4]

                def fn(t, sl=slot, b=bank, ic_=ic, em=dict(xne_by_e)):
                    wsl = ws[sl].ap().rearrange("p (e dc i) -> p e dc i", e=NE, dc=CH)
                    first = True
                    for e in range(NE):
                        s_, v_ = em[e]
                        t.wait_ge(s_.h, v_)
                        rhs = xne_ap(e).rearrange("p (c t) -> p c t", c=CH)
                        for dc in range(CH):
                            t.matmul(b[:], wsl[:, e, dc, :], rhs[:, dc], start=first, stop=False)
                            first = False
                    return t.matmul(b[:], b1b[:, ic_ * 128:(ic_ + 1) * 128], masks4[:],
                                    start=False, stop=True)
                gm = emit("tensor", fn,
                          waits=[mku, bank_free[ic % 4], M["masks4"], m_prep_all],
                          inc=sMM)
                w1_last = gm
                slot_busy[slot] = gm
                hfree = M["qscan"] if ic < 16 else M["e"]
                ev = emit("scalar",
                          lambda s, b=bank, ic_=ic: s.activation(hview(ic_), b[:], AF.Gelu),
                          waits=[gm, hfree], inc=sAC)
                h_ev.append(ev)
                bank_free[ic % 4] = ev
            M["w1_last"] = w1_last

            def s_evac(gm, ec, bank):
                return emit("vector",
                            lambda v, b=bank, e=ec: v.tensor_scalar(v3[:, e], b[:], 0.5, None, ALU.mult),
                            waits=[gm, M["delta"], M["wkv"]], inc=sDV)
            proj_groups("o", sb13, M["ybf"], s_evac)

            fy_ev = []
            wo2_last = None
            for oc in range(CH):
                load_unit((ti, "wo2", oc), cwo2_h[oc].rearrange("p ic o -> p (ic o)"),
                          m_prep[("wo2h", oc // 4)])
                mku, slot = unit_loaded[(ti, "wo2", oc)]
                bank = pbank[oc % 4]

                def fn(t, sl=slot, b=bank):
                    wsl = ws[sl].ap().rearrange("p (ic o) -> p ic o", ic=NIC)
                    for ic in range(NIC):
                        ins = t.matmul(b[:], wsl[:, ic, :], hview(ic),
                                       start=(ic == 0), stop=(ic == NIC - 1))
                    return ins
                gm = emit("tensor", fn, waits=[mku, bank_free[oc % 4], h_ev[-1]], inc=sMM)
                wo2_last = gm
                slot_busy[slot] = gm
                ev = emit("scalar",
                          lambda s, b=bank, o=oc: s.activation(r3[:, o], b[:], AF.Copy),
                          waits=[gm, M["ybf"], m_vecs], inc=sAC)
                fy_ev.append(ev)
                bank_free[oc % 4] = ev
            M["wo2_last"] = wo2_last

            res_marks = []
            y2_marks = []
            ysq_marks = []
            for c in range(CH):
                rm = emit("vector",
                          lambda v, c_=c: v.scalar_tensor_tensor(r3[:, c_], r3[:, c_], vcol(V_BO, c_),
                                                                 xn3[:, c_], ALU.add, ALU.add),
                          waits=[fy_ev[c], M["xnl"], m_vecs, M["oev7"]], inc=sDV)
                res_marks.append(rm)
                y2_marks.append(emit("vector",
                                     lambda v, c_=c: v.tensor_copy(sb13[:, c_], r3[:, c_]),
                                     waits=[rm], inc=sDV))
                ysq_marks.append(emit("scalar",
                                      lambda s, c_=c: s.activation(sb23[:, c_], r3[:, c_], AF.Square),
                                      waits=[rm, M["router"], M["w1_last"], M["mul_e1"]], inc=sAC))
            M["ffn_res"] = res_marks[-1]

            M["y2bf"] = y2_marks[-1]
            M["ysq"] = ysq_marks[-1]

            def stats_group_w(dst, rhs3, cmarks):
                def fn(t):
                    for c in range(CH):
                        s_, v_ = cmarks[c]
                        t.wait_ge(s_.h, v_)
                        ins = t.matmul(dst[0:1, :], onesK[:], rhs3[:, c],
                                       start=(c == 0), stop=(c == CH - 1))
                    return ins
                return fn
            M["ln2_sum1"] = emit("tensor", stats_group_w(prow0, sb13, y2_marks),
                                 waits=[M["lg4"]], inc=sMM)
            M["ln2_sum2"] = emit("tensor", stats_group_w(prow1, sb23, ysq_marks),
                                 waits=[M["masks4"]], inc=sMM)
            _ra2 = emit("vector", rows_a,
                        waits=[M["ln2_sum1"], M["ln2_sum2"], M["m4bc"]], inc=sDV)
            M["var2"] = emit("vector", rows_b, waits=[_ra2], inc=sDV)
            M["rstd2"] = emit("scalar", rstd_a(epsb[0:1, 1:2]), waits=[M["var2"]], inc=sAC)
            M["bc_mu2"] = emit("tensor",
                               lambda t: t.matmul(pbc0[:], onesF[:], rowf[0:1, R_MU:R_MU + TT],
                                                  start=True, stop=True),
                               waits=[M["rstd2"], M["xn"], M["ev_e3"]], inc=sMM)
            M["bc_rs2"] = emit("tensor",
                               lambda t: t.matmul(pbc1[:], onesF[:], rowf[0:1, R_RSTD:R_RSTD + TT],
                                                  start=True, stop=True),
                               waits=[M["rstd2"], M["xn"], M["mul_e1"]], inc=sMM)
            M["xrld"] = emit("sync", lambda s, sp=xsrc: s.dma_start(out=x3, in_=sp),
                             waits=[(sMM, sMM.n), M["tau"]], inc=sX, amt=16)

            out_marks = []
            ydst = y_h.rearrange("(c p) t -> p c t", p=128)[:, :, t0:t0 + TT]
            st_marks = []
            for hw in range(2):
                cs, ce = hw * 4, hw * 4 + 4
                def _tl(v, a=cs, b=ce):
                    for c in range(a, b):
                        v.tensor_tensor(P3[:, c, 0:TT], r3[:, c], pbc0[:], ALU.subtract)
                    for c in range(a, b):
                        ins = v.tensor_tensor(P3[:, c, 0:TT], P3[:, c, 0:TT], pbc1[:], ALU.mult)
                    return ins
                tl = emit("vector", _tl, waits=[M["bc_mu2"], M["bc_rs2"], M["carry"]], inc=sDV)

                def _zm(v, a=cs, b=ce):
                    for c in range(a, b):
                        ins = v.tensor_scalar(Q3[:, c, 0:TT], P3[:, c, 0:TT],
                                              vcol(V_GAM, c), vcol(V_BET, c), ALU.mult, ALU.add)
                    return ins
                zm = emit("vector", _zm, waits=[tl], inc=sDV)
                t2m = emit("scalar",
                           lambda s, a=cs, b=ce: s.activation(B3[:, a:b, :], Q3[:, a:b, 0:TT], AF.Tanh,
                                                              bias=0.0, scale=0.5),
                           waits=[zm, M["ybf"]], inc=sAC)
                e2m = emit(EPOOL,
                           lambda v, a=cs, b=ce: v.tensor_tensor(P3[:, a:b, 0:TT], Q3[:, a:b, 0:TT],
                                                                 B3[:, a:b, :], ALU.mult),
                           waits=[t2m], inc=sEP)
                g2m = emit(EPOOL,
                           lambda v, a=cs, b=ce: v.tensor_tensor(B3[:, a:b, :], Q3[:, a:b, 0:TT],
                                                                 P3[:, a:b, 0:TT], ALU.add),
                           waits=[e2m], inc=sEP)
                t3m = emit("vector",
                           lambda v, a=cs, b=ce: v.tensor_tensor(Q3[:, a:b, 0:TT], v3[:, a:b, :],
                                                                 B3[:, a:b, :], ALU.mult),
                           waits=[g2m, M["oev7"]], inc=sDV)
                om = emit(EPOOL,
                          lambda v, a=cs, b=ce: v.tensor_tensor(P3[:, a:b, 0:TT], x3[:, a:b],
                                                                Q3[:, a:b, 0:TT], ALU.add),
                          waits=[t3m, M["xrld"]], inc=sEP)
                out_marks.append(om)
                M[f"out_w{hw}"] = om
                st_marks.append(emit("sync",
                                     lambda s, a=cs, b=ce, yd=ydst: s.dma_start(out=yd[:, a:b], in_=P3[:, a:b, 0:TT]),
                                     waits=[om], inc=sST, amt=16))
                M["t_ln2"] = tl
                M["z"] = zm
                M["g2"] = g2m
                M["t3"] = t3m
            M["out"] = out_marks[-1]
            M["store"] = st_marks[-1]

            m_prev_out = M["out"]
            m_prev_store = M["store"]
            m_prev_t3 = M["t3"]
            mt = {"ln2_sum1": M["ln2_sum1"], "ln2_sum2": M["ln2_sum2"],
                  "ffn_res": M["ffn_res"], "w1_last": M["w1_last"], "t3": M["t3"],
                  "t_ln2": M["t_ln2"], "wo2_last": M["wo2_last"], "f": M["f"],
                  "masks4": M["masks4"], "mrows": M["mrows"],
                  "maskbc_last": M["bc_e3"], "rowf_guard": M["bc_rs2"],
                  "out_w0": M["out_w0"], "out_w1": M["out_w1"],
                  "pbc_guard": M["t_ln2"], "rowread0": M["var2"], "rowread1": M["var2"]}

        emit("sync", lambda s: s.nop(), waits=[(sST, sST.n)])

        def runner(name):
            cache = {}

            def run(eng):
                for waits, fn, inch, amt in prog[name]:
                    for (s, v) in waits:
                        if v > 0 and cache.get(id(s), 0) < v:
                            eng.wait_ge(s.h, v)
                            cache[id(s)] = v
                    ins = fn(eng)
                    if inch is not None:
                        ins.then_inc(inch, amt)
            return run

        block.sync(runner("sync"))
        block.vector(runner("vector"))
        block.scalar(runner("scalar"))
        block.tensor(runner("tensor"))
        block.gpsimd(runner("gpsimd"))

    return nc


def kernel(**inputs):
    x = np.asarray(inputs["L0_attn_x"], dtype=np.float32)
    xT = np.ascontiguousarray(np.transpose(x, (0, 2, 1)))
    wkT = np.ascontiguousarray(np.asarray(inputs["L0_attn_W_k"], np.float32).T)
    wvT = np.ascontiguousarray(np.asarray(inputs["L0_attn_W_v"], np.float32).T)
    wrT = np.ascontiguousarray(np.asarray(inputs["L0_attn_W_r"], np.float32).T)
    woT = np.ascontiguousarray(np.asarray(inputs["L0_attn_W_o"], np.float32).T)
    w1 = np.ascontiguousarray(
        np.asarray(inputs["L0_ffn_W1"], np.float32)
        .reshape(NE, CH, 128, NIC, 128).transpose(3, 2, 0, 1, 4))
    wo2 = np.ascontiguousarray(
        np.asarray(inputs["L0_ffn_Wo"], np.float32)
        .reshape(NIC, 128, CH, 128).transpose(2, 1, 0, 3))
    b1 = np.ascontiguousarray(np.asarray(inputs["L0_ffn_b1"], np.float32))
    wrt = np.ascontiguousarray(
        np.asarray(inputs["L0_ffn_W_router"], np.float32)
        .reshape(CH, 128, NE).transpose(1, 0, 2).reshape(128, CH * NE))
    vec_list = [inputs["L0_attn_time_mix_k"], inputs["L0_attn_time_mix_v"],
                inputs["L0_attn_time_mix_r"], inputs["L0_attn_time_decay"],
                inputs["L0_attn_time_first"], inputs["L0_ffn_ln_gamma"],
                inputs["L0_ffn_ln_beta"], inputs["L0_ffn_bo"]]
    vecs = np.ascontiguousarray(
        np.stack([np.asarray(v, np.float32) for v in vec_list])
        .reshape(8, CH, 128).transpose(2, 0, 1).reshape(128, 64))

    nc = build()
    shared = {"wkT": wkT, "wvT": wvT, "wrT": wrT, "woT": woT, "w1": w1,
              "wo2": wo2, "b1": b1, "wrt": wrt, "vecs_in": vecs}
    in_maps = [dict(shared, x=np.ascontiguousarray(xT[c])) for c in range(8)]
    res = run_bass_kernel_spmd(nc, in_maps, core_ids=list(range(8)))
    out = np.stack([np.ascontiguousarray(np.asarray(r["y"], np.float32).T) for r in res.results])
    return out.astype(np.float32)


if __name__ == "__main__":
    import reference as _r
    inp = _r.setup_inputs()
    inp = {k: np.asarray(v) for k, v in inp.items()}
    got = kernel(**inp)
    ref = np.asarray(_r.reference(**inp))
    rv = float(np.mean((got - ref) ** 2) / np.mean(ref ** 2))
    print("resid-var ratio:", rv)


# version 17
# speedup vs baseline: 1.0142x; 1.0142x over previous
import numpy as np
import ml_dtypes
import concourse.bass as bass
import concourse.mybir as mybir
from concourse.bass_utils import run_bass_kernel_spmd
from contextlib import ExitStack

F32 = mybir.dt.float32
BF16 = mybir.dt.bfloat16
AF = mybir.ActivationFunctionType
ALU = mybir.AluOpType

D = 1024
T = 2048
CH = 8
TT = 512
NT = T // TT
I_FF = 4096
NE = 4
NIC = I_FF // 128

V_TMK, V_TMV, V_TMR, V_W, V_U, V_GAM, V_BET, V_BO = range(8)
R_MU, R_RSTD, R_LNT, R_M, R_LROW = 0, 512, 1024, 512, 0


class Sem:
    def __init__(self, ctx, nc, name):
        self.h = ctx.enter_context(nc.semaphore(name))
        self.n = 0

    def bump(self, amt):
        self.n += amt
        return (self, self.n)


def build():
    nc = bass.Bass("TRN2", target_bir_lowering=False, dynamic_dma_scratch_size=4096,
                   detect_race_conditions=False)

    x_h = nc.dram_tensor("x", [D, T], F32, kind="ExternalInput").ap()
    wk_h = nc.dram_tensor("wkT", [D, D], F32, kind="ExternalInput").ap()
    wv_h = nc.dram_tensor("wvT", [D, D], F32, kind="ExternalInput").ap()
    wr_h = nc.dram_tensor("wrT", [D, D], F32, kind="ExternalInput").ap()
    wo_h = nc.dram_tensor("woT", [D, D], F32, kind="ExternalInput").ap()
    w1_h = nc.dram_tensor("w1", [NIC, 128, NE, CH, 128], F32, kind="ExternalInput").ap()
    wo2_h = nc.dram_tensor("wo2", [CH, 128, NIC, 128], F32, kind="ExternalInput").ap()
    b1_h = nc.dram_tensor("b1", [NE, I_FF], F32, kind="ExternalInput").ap()
    wrt_h = nc.dram_tensor("wrt", [128, CH * NE], F32, kind="ExternalInput").ap()
    vecs_h = nc.dram_tensor("vecs_in", [128, 64], F32, kind="ExternalInput").ap()
    y_h = nc.dram_tensor("y", [D, T], F32, kind="ExternalOutput").ap()

    ck_h = nc.dram_tensor("ck", [D, D], BF16, kind="Internal").ap()
    cv_h = nc.dram_tensor("cv", [D, D], BF16, kind="Internal").ap()
    cr_h = nc.dram_tensor("cr", [D, D], BF16, kind="Internal").ap()
    co_h = nc.dram_tensor("co", [D, D], BF16, kind="Internal").ap()
    cw1_h = nc.dram_tensor("cw1", [NIC, 128, NE, CH, 128], BF16, kind="Internal").ap()
    cwo2_h = nc.dram_tensor("cwo2", [CH, 128, NIC, 128], BF16, kind="Internal").ap()

    with ExitStack() as ctx:
        def sb(name, shp, dt=F32):
            return ctx.enter_context(nc.sbuf_tensor(name, shp, dt))

        xb = sb("xb", [128, CH * TT])
        xn = sb("xn", [128, CH * TT])
        kb = sb("kb", [128, CH * TT])
        vb = sb("vb", [128, CH * TT])
        rb = sb("rb", [128, CH * TT])
        Ab = sb("Ab", [128, CH * TT])
        Bb = sb("Bb", [128, CH * TT])
        Pb = sb("Pb", [128, CH * (TT + 1)])
        Qb = sb("Qb", [128, CH * (TT + 1)])
        sbf1 = sb("sbf1", [128, CH * TT], BF16)
        sbf2 = sb("sbf2", [128, CH * TT], BF16)
        sbf3 = sb("sbf3", [128, CH * TT], BF16)
        sbf5 = sb("sbf5", [128, CH * TT], BF16)
        ws = [sb("ws0", [128, 4096], BF16), sb("ws1", [128, 4096], BF16)]
        b1b = sb("b1b", [NE, I_FF], BF16)
        wrtb = sb("wrtb", [128, CH * NE], BF16)
        vecs = sb("vecs", [128, 64], F32)
        pc = sb("pc", [128, CH], F32)
        qc = sb("qc", [128, CH], F32)
        xnl = sb("xnl", [128, CH], F32)
        rowf = sb("rowf", [1, 2048], F32)
        masks4 = sb("masks4", [NE, TT], BF16)
        onesK = sb("onesK", [128, 1], BF16)
        onesrB = sb("onesrB", [1, 128], BF16)
        onesF = sb("onesF", [1, 128], F32)
        ones4F = sb("ones4F", [1, NE], F32)
        epsb = sb("epsb", [1, 2], F32)

        pbank = [ctx.enter_context(nc.psum_tensor(f"pb{i}", [128, TT], F32)) for i in range(4)]
        pbc0 = ctx.enter_context(nc.psum_tensor("pbc0", [128, TT], F32))
        pbc1 = ctx.enter_context(nc.psum_tensor("pbc1", [128, TT], F32))
        prow0 = ctx.enter_context(nc.psum_tensor("prow0", [NE, TT], F32))
        prow1 = ctx.enter_context(nc.psum_tensor("prow1", [NE, TT], F32))

        sP = Sem(ctx, nc, "sP")
        sG = [Sem(ctx, nc, "sG0"), Sem(ctx, nc, "sG1")]
        sU = [Sem(ctx, nc, "sU0"), Sem(ctx, nc, "sU1")]
        sS = [Sem(ctx, nc, "sS0"), Sem(ctx, nc, "sS1")]
        sV = Sem(ctx, nc, "sV")
        sX = Sem(ctx, nc, "sX")
        sR = Sem(ctx, nc, "sR")
        sW = Sem(ctx, nc, "sW")
        sMM = Sem(ctx, nc, "sMM")
        sDV = Sem(ctx, nc, "sDV")
        sAC = Sem(ctx, nc, "sAC")
        sPL = Sem(ctx, nc, "sPL")
        sST = Sem(ctx, nc, "sST")

        block = ctx.enter_context(nc.Block())
        prog = {k: [] for k in ("sync", "vector", "scalar", "tensor", "gpsimd")}

        def emit(ename, fn, waits=(), inc=None, amt=1):
            mark = inc.bump(amt) if inc is not None else None
            prog[ename].append((tuple(w for w in waits if w is not None), fn,
                                inc.h if inc else None, amt))
            return mark

        def c3(t):
            return t.ap().rearrange("p (c t) -> p c t", c=CH)

        x3 = c3(xb); xn3 = c3(xn); k3 = c3(kb); v3 = c3(vb); r3 = c3(rb)
        A3 = c3(Ab); B3 = c3(Bb)
        P3 = Pb.ap().rearrange("p (c t) -> p c t", c=CH)
        Q3 = Qb.ap().rearrange("p (c t) -> p c t", c=CH)
        Ppre = P3[:, :, 0:TT]; Pcur = P3[:, :, 1:TT + 1]
        Qpre = Q3[:, :, 0:TT]
        sb13 = c3(sbf1); sb23 = c3(sbf2); sb33 = c3(sbf3)
        kbf = kb.ap().bitcast(BF16)
        Abf = Ab.ap().bitcast(BF16)
        Bbf = Bb.ap().bitcast(BF16)
        xbbf = xb.ap().bitcast(BF16)

        rbbf = rb.ap().bitcast(BF16)
        mixv3 = Bbf[:, 0:CH * TT].rearrange("p (c t) -> p c t", c=CH)
        mixr3 = Bbf[:, CH * TT:2 * CH * TT].rearrange("p (c t) -> p c t", c=CH)
        mrow = sbf5.ap()[0:1, 2048:4096]
        logits4 = Pb.ap()[0:4, 0:TT]
        rbf3 = rbbf[:, 0:CH * TT].rearrange("p (c t) -> p c t", c=CH)

        def xne_ap(e):
            return [sbf3.ap(), sbf2.ap(), sbf5.ap(),
                    rbbf[:, CH * TT:2 * CH * TT]][e]

        def hview(ic):
            if ic < 16:
                return Abf[:, ic * TT:(ic + 1) * TT]
            return xbbf[:, (ic - 16) * TT:(ic - 15) * TT]

        def vcol(v, c):
            return vecs[:, v * CH + c:v * CH + c + 1]

        m_prep = {}
        prev_prep = [None]

        def prep(name, fn):
            mk = emit("gpsimd", fn, waits=[prev_prep[0]], inc=sP, amt=16)
            prev_prep[0] = mk
            m_prep[name] = mk
            return mk
        prep("k", lambda g: g.dma_start(out=ck_h, in_=wk_h))
        prep("v", lambda g: g.dma_start(out=cv_h, in_=wv_h))
        prep("r", lambda g: g.dma_start(out=cr_h, in_=wr_h))
        prep("wrt", lambda g: g.dma_start(out=wrtb[:], in_=wrt_h))
        prep("b1", lambda g: g.dma_start(out=b1b[:], in_=b1_h))
        w1q = list(range(16))
        for j in w1q[:2]:
            prep(("w1q", j), lambda g, j_=j: g.dma_start(out=cw1_h[2 * j_:2 * (j_ + 1)],
                                                         in_=w1_h[2 * j_:2 * (j_ + 1)]))
        prep("o", lambda g: g.dma_start(out=co_h, in_=wo_h))
        for j in w1q[2:]:
            prep(("w1q", j), lambda g, j_=j: g.dma_start(out=cw1_h[2 * j_:2 * (j_ + 1)],
                                                         in_=w1_h[2 * j_:2 * (j_ + 1)]))
        prep(("wo2h", 0), lambda g: g.dma_start(out=cwo2_h[0:4], in_=wo2_h[0:4]))
        prep(("wo2h", 1), lambda g: g.dma_start(out=cwo2_h[4:8], in_=wo2_h[4:8]))
        m_prep_all = m_prep["b1"]
        emit("sync", lambda s: s.dma_start(out=vecs[:], in_=vecs_h), inc=sV, amt=16)
        m_vecs = (sV, sV.n)

        def _init(v):
            v.memset(pc[:], -1e30)
            v.memset(qc[:], 0.0)
            v.memset(xnl[:], 0.0)
            v.memset(onesK[:], 1.0)
            v.memset(onesrB[:], 1.0)
            v.memset(onesF[:], 1.0)
            v.memset(ones4F[:], 1.0)
            v.memset(epsb[0:1, 0:1], 1e-5)
            return v.memset(epsb[0:1, 1:2], 1e-12)
        emit("vector", _init, inc=sDV)
        m_init = (sDV, sDV.n)

        slot_busy = [None, None]
        unit_loaded = {}
        unit_ctr = [0]
        bank_free = [None, None, None, None]

        def load_unit(uid, src_ap, prep_mark, iu=None):
            n = unit_ctr[0]; unit_ctr[0] += 1
            slot = n % 2
            mk = emit("sync", lambda s, sl=slot, sp=src_ap: s.dma_start(out=ws[sl][:], in_=sp),
                      waits=[prep_mark, slot_busy[slot]], inc=sU[slot], amt=16)
            unit_loaded[uid] = (mk, slot)
            return slot

        ckr = {"k": ck_h.rearrange("(dc dp) e -> dp dc e", dp=128),
               "v": cv_h.rearrange("(dc dp) e -> dp dc e", dp=128),
               "r": cr_h.rearrange("(dc dp) e -> dp dc e", dp=128),
               "o": co_h.rearrange("(dc dp) e -> dp dc e", dp=128)}

        m_prev_out = None
        m_prev_store = None
        m_prev_t3 = None
        mt = {}

        for ti in range(NT):
            EPOOL = "vector" if ti == 0 else "gpsimd"
            sEP = sDV if EPOOL == "vector" else sPL
            t0 = ti * TT
            M = {}

            xsrc = x_h.rearrange("(c p) t -> p c t", p=128)[:, :, t0:t0 + TT]
            xld_h = []
            for hw in range(2):
                xld_h.append(emit("sync",
                             lambda s, sp=xsrc, a=hw * 4, b=hw * 4 + 4: s.dma_start(out=x3[:, a:b], in_=sp[:, a:b]),
                             waits=[mt.get(f"out_w{hw}")], inc=sX, amt=16))
            M["xld"] = xld_h[-1]

            xbf_h = []
            xsq_h = []
            for hw in range(2):
                a, bnd = hw * 4, hw * 4 + 4
                xbf_h.append(emit("vector",
                             lambda v, a_=a, b_=bnd: v.tensor_copy(sb13[:, a_:b_].rearrange("p c t -> p (c t)"), x3[:, a_:b_].rearrange("p c t -> p (c t)")),
                             waits=[xld_h[hw], mt.get("ln2_sum1"), m_init], inc=sDV))
                xsq_h.append(emit("scalar",
                             lambda s, a_=a, b_=bnd: s.activation(sb23[:, a_:b_], x3[:, a_:b_], AF.Square),
                             waits=[xld_h[hw], mt.get("ln2_sum2"), mt.get("w1_last")], inc=sAC))
            M["xbf"] = xbf_h[-1]
            M["xsq"] = xsq_h[-1]

            def stats_group(dst, rhs3, hmarks=None):
                def fn(t):
                    for c in range(CH):
                        if hmarks is not None:
                            s_, v_ = hmarks[c // 4]
                            t.wait_ge(s_.h, v_)
                        ins = t.matmul(dst[0:1, :], onesK[:], rhs3[:, c],
                                       start=(c == 0), stop=(c == CH - 1))
                    return ins
                return fn
            M["sum1"] = emit("tensor", stats_group(prow0, sb13, xbf_h),
                             waits=[mt.get("rowread0")], inc=sMM)
            M["sum2"] = emit("tensor", stats_group(prow1, sb23, xsq_h),
                             waits=[mt.get("rowread1")], inc=sMM)

            def rows_a(v):
                v.tensor_scalar(rowf[:, R_MU:R_MU + TT], prow0[0:1, :], 1.0 / D, None, ALU.mult)
                return v.tensor_scalar(rowf[:, R_RSTD:R_RSTD + TT], prow1[0:1, :], 1.0 / D, None, ALU.mult)

            def rows_b(v):
                v.tensor_tensor(rowf[:, R_LNT:R_LNT + TT], rowf[:, R_MU:R_MU + TT],
                                rowf[:, R_MU:R_MU + TT], ALU.mult)
                return v.tensor_tensor(rowf[:, R_RSTD:R_RSTD + TT], rowf[:, R_RSTD:R_RSTD + TT],
                                       rowf[:, R_LNT:R_LNT + TT], ALU.subtract)
            _ra1 = emit("vector", rows_a,
                        waits=[M["sum1"], M["sum2"], mt.get("rowf_guard")], inc=sDV)
            M["var1"] = emit("vector", rows_b, waits=[_ra1], inc=sDV)

            def rstd_a(eps):
                def fn(s):
                    s.activation(rowf[:, R_LNT:R_LNT + TT], rowf[:, R_RSTD:R_RSTD + TT],
                                 AF.Ln, bias=eps, scale=1.0)
                    return s.activation(rowf[:, R_RSTD:R_RSTD + TT], rowf[:, R_LNT:R_LNT + TT],
                                        AF.Exp, bias=0.0, scale=-0.5)
                return fn
            M["rstd1"] = emit("scalar", rstd_a(epsb[0:1, 0:1]), waits=[M["var1"], m_init], inc=sAC)

            M["bc_mu1"] = emit("tensor",
                               lambda t: t.matmul(pbc0[:], onesF[:], rowf[0:1, R_MU:R_MU + TT],
                                                  start=True, stop=True),
                               waits=[M["rstd1"], mt.get("pbc_guard")], inc=sMM)
            M["bc_rs1"] = emit("tensor",
                               lambda t: t.matmul(pbc1[:], onesF[:], rowf[0:1, R_RSTD:R_RSTD + TT],
                                                  start=True, stop=True),
                               waits=[M["rstd1"], mt.get("pbc_guard")], inc=sMM)

            def apply_ln1(v):
                for c in range(CH):
                    v.tensor_tensor(xn3[:, c], x3[:, c], pbc0[:], ALU.subtract)
                for c in range(CH):
                    ins = v.tensor_tensor(xn3[:, c], xn3[:, c], pbc1[:], ALU.mult)
                return ins
            M["xn"] = emit("vector", apply_ln1,
                           waits=[M["bc_mu1"], M["bc_rs1"], mt.get("ffn_res")], inc=sDV)
            M["xnbf"] = emit("vector", lambda v: v.tensor_copy(sbf2[:], xn[:]),
                             waits=[M["xsq"]], inc=sDV)

            def router(t):
                for c in range(CH):
                    ins = t.matmul(prow0[0:NE, :], wrtb[:, c * NE:(c + 1) * NE], sb23[:, c],
                                   start=(c == 0), stop=(c == CH - 1))
                return ins
            M["router"] = emit("tensor", router,
                               waits=[M["xnbf"], m_prep_all, M["var1"]], inc=sMM)

            def diffs(v):
                v.tensor_tensor(A3[:, :, 1:], xn3[:, :, 1:], xn3[:, :, 0:TT - 1], ALU.subtract)
                return v.tensor_tensor(A3[:, :, 0], xn3[:, :, 0], xnl.ap(), ALU.subtract)
            M["diff"] = emit("vector", diffs, waits=[M["xn"], mt.get("wo2_last")], inc=sDV)

            def mk_mix(dst3, vidx):
                def fn(v):
                    for c in range(CH):
                        v.scalar_tensor_tensor(dst3[:, c, 1:], A3[:, c, 1:], vcol(vidx, c),
                                               xn3[:, c, 0:TT - 1], ALU.mult, ALU.add)
                        ins = v.scalar_tensor_tensor(dst3[:, c, 0:1], A3[:, c, 0:1], vcol(vidx, c),
                                                     xnl[:, c:c + 1], ALU.mult, ALU.add)
                    return ins
                return fn
            M["mixk"] = emit("vector", mk_mix(sb33, V_TMK),
                             waits=[M["diff"], m_vecs, mt.get("w1_last")], inc=sDV)
            M["mixv"] = emit("vector", mk_mix(mixv3, V_TMV),
                             waits=[M["diff"], m_vecs, mt.get("t3")], inc=sDV)
            M["mixr"] = emit("vector", mk_mix(mixr3, V_TMR),
                             waits=[M["diff"], m_vecs], inc=sDV)

            IU = {"k": 0, "v": 2, "r": 4, "o": 38}

            def proj_groups(nm, rhs3, mixmark, evac):
                load_unit((ti, nm, 0), ckr[nm][:, :, 0:512], m_prep[nm])
                load_unit((ti, nm, 1), ckr[nm][:, :, 512:1024], m_prep[nm])
                for ec in range(CH):
                    mku, slot = unit_loaded[(ti, nm, 0 if ec < 4 else 1)]
                    bank = pbank[ec % 4]
                    wsl = ws[slot].ap().rearrange("p (dc e) -> p dc e", dc=CH)
                    eo = (ec % 4) * 128

                    def fn(t, w=wsl, b=bank, eo_=eo, r3_=rhs3):
                        for dc in range(CH):
                            ins = t.matmul(b[:], w[:, dc, eo_:eo_ + 128], r3_[:, dc],
                                           start=(dc == 0), stop=(dc == CH - 1))
                        return ins
                    gm = emit("tensor", fn, waits=[mku, mixmark, bank_free[ec % 4]], inc=sMM)
                    M[f"{nm}grp{ec}"] = gm
                    ev = evac(gm, ec, bank)
                    M[f"{nm}ev{ec}"] = ev
                    bank_free[ec % 4] = ev
                    if ec == 3:
                        slot_busy[unit_loaded[(ti, nm, 0)][1]] = gm
                    if ec == 7:
                        slot_busy[unit_loaded[(ti, nm, 1)][1]] = gm

            proj_groups("k", sb33, M["mixk"],
                        lambda gm, ec, bank: emit("scalar",
                            lambda s, b=bank, e=ec: s.activation(k3[:, e], b[:], AF.Copy),
                            waits=[gm, mt.get("f")], inc=sAC))
            M["lg4"] = emit("vector", lambda v: v.tensor_copy(logits4, prow0[0:NE, :]),
                            waits=[M["router"], mt.get("masks4"), m_prev_store], inc=sDV)
            proj_groups("v", mixv3, M["mixv"],
                        lambda gm, ec, bank: emit("vector",
                            lambda v, b=bank, e=ec: v.tensor_copy(v3[:, e], b[:]),
                            waits=[gm, mt.get("t3")], inc=sDV))

            M["realign"] = emit("sync", lambda s: s.dma_start(
                out=rowf.ap()[0:1, R_LROW:R_LROW + NE * TT].rearrange("p (e t) -> p e t", e=NE),
                in_=logits4), waits=[M["lg4"], mt.get("mrows"), M["bc_mu1"], M["bc_rs1"]],
                inc=sR, amt=16)

            def maxes(v):
                L = rowf[0:1, :]
                v.tensor_tensor(L[:, 0:TT], L[:, 0:TT], L[:, TT:2 * TT], ALU.max)
                v.tensor_tensor(L[:, 2 * TT:3 * TT], L[:, 2 * TT:3 * TT], L[:, 3 * TT:4 * TT], ALU.max)
                return v.tensor_tensor(L[:, R_M:R_M + TT], L[:, 0:TT], L[:, 2 * TT:3 * TT], ALU.max)
            M["mrows"] = emit("vector", maxes,
                              waits=[M["realign"], M["rstd1"], M["bc_mu1"], M["bc_rs1"]],
                              inc=sDV)
            proj_groups("r", mixr3, M["mixr"],
                        lambda gm, ec, bank: emit("scalar",
                            lambda s, b=bank, e=ec: s.activation(rbf3[:, e], b[:], AF.Tanh,
                                                                 bias=0.0, scale=0.5),
                            waits=[gm, mt.get("t_ln2")], inc=sAC))
            M["xnl"] = emit("vector", lambda v: v.tensor_copy(xnl[:], xn3[:, :, TT - 1]),
                            waits=[M["mixv"], M["mixr"]], inc=sDV)
            M["rfix"] = emit("vector",
                             lambda v: v.tensor_scalar(rbbf[:, 0:CH * TT], rbbf[:, 0:CH * TT],
                                                       0.5, 0.5, ALU.mult, ALU.add),
                             waits=[M["rev7"]], inc=sDV)

            M["m4bc"] = emit("tensor",
                             lambda t: t.matmul(prow1[0:NE, :], ones4F[:], rowf[0:1, R_M:R_M + TT],
                                                start=True, stop=True),
                             waits=[M["mrows"], M["var1"]], inc=sMM)
            M["masks4"] = emit("vector",
                               lambda v: v.tensor_tensor(masks4[:], logits4, prow1[0:NE, :], ALU.is_equal),
                               waits=[M["m4bc"]], inc=sDV)
            M["mrowre"] = emit("sync", lambda s: s.dma_start(
                out=mrow.rearrange("p (e t) -> p e t", e=NE), in_=masks4[:]),
                waits=[M["masks4"], mt.get("maskbc_last"), mt.get("w1_last")], inc=sR, amt=16)

            def bc_mask(e, dst, extra):
                return emit("tensor",
                            lambda t, e_=e, d=dst: t.matmul(d[:], onesrB[:],
                                                            mrow[:, e_ * TT:(e_ + 1) * TT],
                                                            start=True, stop=True),
                            waits=[M["mrowre"], M["xn"]] + extra, inc=sMM)

            def ev_mask(eng, dstbuf, psum, extra):
                def fn(s, d=dstbuf, ps=psum):
                    d3 = d.rearrange("p (c t) -> p c t", c=CH)
                    for c in range(CH):
                        ins = (s.activation(d3[:, c], ps[:], AF.Copy) if eng == "scalar"
                               else s.tensor_copy(d3[:, c], ps[:]))
                    return ins
                return emit(eng, fn, waits=extra, inc=(sAC if eng == "scalar" else sDV))

            M["bc_e0"] = bc_mask(0, pbc0, [])
            M["bc_e1"] = bc_mask(1, pbc1, [])
            M["ev_e0"] = ev_mask("scalar", xne_ap(0), pbc0, [M["bc_e0"], M["rgrp7"], mt.get("w1_last")])
            M["bc_e3"] = bc_mask(3, pbank[0], [bank_free[0]])
            M["ev_e3"] = ev_mask("vector", xne_ap(3), pbank[0], [M["bc_e3"], mt.get("t_ln2")])
            bank_free[0] = M["ev_e3"]
            M["bc_e2"] = bc_mask(2, pbc0, [M["ev_e0"]])
            M["ev_e2"] = ev_mask("scalar", xne_ap(2), pbc0, [M["bc_e2"], mt.get("w1_last")])

            xne_by_e = {}
            for e, eng in ((0, EPOOL), (2, EPOOL), (3, "vector")):
                def fn(v, e_=e):
                    dst = xne_ap(e_).rearrange("p (c t) -> p c t", c=CH)
                    for c in range(CH):
                        ins = v.tensor_tensor(dst[:, c], dst[:, c], sb23[:, c], ALU.mult)
                    return ins
                xne_by_e[e] = emit(eng, fn, waits=[M[f"ev_e{e}"], M["xnbf"]],
                                   inc=(sDV if eng == "vector" else sEP))
            xne_marks = list(xne_by_e.values())

            def mul_e1(v):
                for c in range(CH):
                    ins = v.tensor_tensor(sb23[:, c], sb23[:, c], pbc1[:], ALU.mult)
                return ins
            M["mul_e1"] = emit("vector", mul_e1,
                               waits=[M["bc_e1"], M["router"]] + xne_marks, inc=sDV)
            xne_by_e[1] = M["mul_e1"]

            M["pcin"] = emit("vector",
                             lambda v: (v.tensor_copy(P3[:, :, 0], pc.ap()),
                                        v.tensor_copy(Q3[:, :, 0], qc.ap()))[-1],
                             waits=[m_prev_store, m_prev_out, M["realign"]], inc=sDV)

            def pscan(v):
                for c in range(CH):
                    wb = vcol(V_W, c).broadcast_to([128, TT])
                    ins = v.tensor_tensor_scan(P3[:, c, 1:], wb, k3[:, c], P3[:, c, 0:1],
                                               ALU.add, ALU.max)
                return ins
            M["pscan"] = emit("vector", pscan, waits=[M["pcin"], M["kev7"], M["realign"]], inc=sDV)

            def asub(v):
                for c in range(CH):
                    ins = v.scalar_tensor_tensor(A3[:, c], Ppre[:, c], vcol(V_W, c), Pcur[:, c],
                                                 ALU.add, ALU.subtract)
                return ins
            M["asub"] = emit("vector", asub, waits=[M["pscan"], M["mixr"]], inc=sDV)
            M["aexp"] = emit("scalar", lambda s: s.activation(xb[:], Ab[:], AF.Exp),
                             waits=[M["asub"]], inc=sAC)
            M["bsub"] = emit("vector",
                             lambda v: v.tensor_tensor(B3[:, :, :], k3[:, :, :], Pcur, ALU.subtract),
                             waits=[M["pscan"], M["rgrp7"]], inc=sDV)
            M["bexp"] = emit("scalar", lambda s: s.activation(Ab[:], Bb[:], AF.Exp),
                             waits=[M["bsub"]], inc=sAC)
            M["bmul"] = emit("vector", lambda v: v.tensor_mul(Ab[:], Ab[:], vb[:]),
                             waits=[M["bexp"], M["vev7"]], inc=sDV)

            def _sigarg(v):
                for c in range(CH):
                    ins = v.scalar_tensor_tensor(B3[:, c], k3[:, c], vcol(V_U, c), Ppre[:, c],
                                                 ALU.add, ALU.subtract)
                return ins
            M["sigarg"] = emit("vector", _sigarg, waits=[M["bexp"]], inc=sDV)

            def qscan(v):
                for c in range(CH):
                    ins = v.tensor_tensor_scan(Q3[:, c, 1:], x3[:, c], A3[:, c], Q3[:, c, 0:1],
                                               ALU.mult, ALU.add)
                return ins
            M["qscan"] = emit("vector", qscan, waits=[M["aexp"], M["bmul"]], inc=sDV)
            M["tau"] = emit("scalar", lambda s: s.activation(xb[:], Bb[:], AF.Tanh, bias=0.0, scale=0.5),
                            waits=[M["sigarg"], M["qscan"]], inc=sAC)
            M["delta"] = emit(EPOOL, lambda v: v.tensor_tensor(B3[:, :, :], Qpre, v3[:, :, :], ALU.subtract),
                              waits=[M["qscan"], M["tau"]], inc=sEP)
            M["e"] = emit("vector", lambda v: v.tensor_mul(kb[:], xb[:], Bb[:]),
                          waits=[M["tau"], M["delta"]], inc=sDV)
            M["f"] = emit(EPOOL, lambda v: v.tensor_sub(Bb[:], Bb[:], kb[:]),
                          waits=[M["e"]], inc=sEP)
            M["wkv"] = emit("vector",
                            lambda v: v.scalar_tensor_tensor(Bb[:], Bb[:], 0.5, vb[:], ALU.mult, ALU.add),
                            waits=[M["f"]], inc=sDV)
            M["ybf"] = emit("vector", lambda v: v.tensor_mul(sbf1[:], Bb[:], rbbf[:, 0:CH * TT]),
                            waits=[M["wkv"], M["rfix"], M["sum1"]], inc=sDV)
            M["carry"] = emit("vector",
                              lambda v: (v.tensor_copy(pc[:], P3[:, :, TT]),
                                         v.tensor_copy(qc[:], Q3[:, :, TT]))[-1],
                              waits=[M["qscan"]], inc=sDV)

            h_ev = []
            w1_last = None
            for ic in range(NIC):
                load_unit((ti, "w1", ic), cw1_h[ic].rearrange("p e dc i -> p (e dc i)"),
                          m_prep[("w1q", ic // 2)])
                mku, slot = unit_loaded[(ti, "w1", ic)]
                bank = pbank[ic % 4]

                def fn(t, sl=slot, b=bank, ic_=ic, em=dict(xne_by_e)):
                    wsl = ws[sl].ap().rearrange("p (e dc i) -> p e dc i", e=NE, dc=CH)
                    first = True
                    for e in range(NE):
                        s_, v_ = em[e]
                        t.wait_ge(s_.h, v_)
                        rhs = xne_ap(e).rearrange("p (c t) -> p c t", c=CH)
                        for dc in range(CH):
                            t.matmul(b[:], wsl[:, e, dc, :], rhs[:, dc], start=first, stop=False)
                            first = False
                    return t.matmul(b[:], b1b[:, ic_ * 128:(ic_ + 1) * 128], masks4[:],
                                    start=False, stop=True)
                gm = emit("tensor", fn,
                          waits=[mku, bank_free[ic % 4], M["masks4"], m_prep_all],
                          inc=sMM)
                w1_last = gm
                slot_busy[slot] = gm
                hfree = M["qscan"] if ic < 16 else M["e"]
                ev = emit("scalar",
                          lambda s, b=bank, ic_=ic: s.activation(hview(ic_), b[:], AF.Gelu),
                          waits=[gm, hfree], inc=sAC)
                h_ev.append(ev)
                bank_free[ic % 4] = ev
            M["w1_last"] = w1_last

            def s_evac(gm, ec, bank):
                return emit("vector",
                            lambda v, b=bank, e=ec: v.tensor_scalar(v3[:, e], b[:], 0.5, None, ALU.mult),
                            waits=[gm, M["delta"], M["wkv"]], inc=sDV)
            proj_groups("o", sb13, M["ybf"], s_evac)

            fy_ev = []
            wo2_last = None
            for oc in range(CH):
                load_unit((ti, "wo2", oc), cwo2_h[oc].rearrange("p ic o -> p (ic o)"),
                          m_prep[("wo2h", oc // 4)])
                mku, slot = unit_loaded[(ti, "wo2", oc)]
                bank = pbank[oc % 4]

                def fn(t, sl=slot, b=bank):
                    wsl = ws[sl].ap().rearrange("p (ic o) -> p ic o", ic=NIC)
                    for ic in range(NIC):
                        ins = t.matmul(b[:], wsl[:, ic, :], hview(ic),
                                       start=(ic == 0), stop=(ic == NIC - 1))
                    return ins
                gm = emit("tensor", fn, waits=[mku, bank_free[oc % 4], h_ev[-1]], inc=sMM)
                wo2_last = gm
                slot_busy[slot] = gm
                ev = emit("scalar",
                          lambda s, b=bank, o=oc: s.activation(r3[:, o], b[:], AF.Copy),
                          waits=[gm, M["ybf"], m_vecs], inc=sAC)
                fy_ev.append(ev)
                bank_free[oc % 4] = ev
            M["wo2_last"] = wo2_last

            res_marks = []
            y2_marks = []
            ysq_marks = []
            for c in range(CH):
                rm = emit("vector",
                          lambda v, c_=c: v.scalar_tensor_tensor(r3[:, c_], r3[:, c_], vcol(V_BO, c_),
                                                                 xn3[:, c_], ALU.add, ALU.add),
                          waits=[fy_ev[c], M["xnl"], m_vecs, M["oev7"]], inc=sDV)
                res_marks.append(rm)
                y2_marks.append(emit("vector",
                                     lambda v, c_=c: v.tensor_copy(sb13[:, c_], r3[:, c_]),
                                     waits=[rm], inc=sDV))
                ysq_marks.append(emit("scalar",
                                      lambda s, c_=c: s.activation(sb23[:, c_], r3[:, c_], AF.Square),
                                      waits=[rm, M["router"], M["w1_last"], M["mul_e1"]], inc=sAC))
            M["ffn_res"] = res_marks[-1]

            M["y2bf"] = y2_marks[-1]
            M["ysq"] = ysq_marks[-1]

            def stats_group_w(dst, rhs3, cmarks):
                def fn(t):
                    for c in range(CH):
                        s_, v_ = cmarks[c]
                        t.wait_ge(s_.h, v_)
                        ins = t.matmul(dst[0:1, :], onesK[:], rhs3[:, c],
                                       start=(c == 0), stop=(c == CH - 1))
                    return ins
                return fn
            M["ln2_sum1"] = emit("tensor", stats_group_w(prow0, sb13, y2_marks),
                                 waits=[M["lg4"]], inc=sMM)
            M["ln2_sum2"] = emit("tensor", stats_group_w(prow1, sb23, ysq_marks),
                                 waits=[M["masks4"]], inc=sMM)
            _ra2 = emit("vector", rows_a,
                        waits=[M["ln2_sum1"], M["ln2_sum2"], M["m4bc"]], inc=sDV)
            M["var2"] = emit("vector", rows_b, waits=[_ra2], inc=sDV)
            M["rstd2"] = emit("scalar", rstd_a(epsb[0:1, 1:2]), waits=[M["var2"]], inc=sAC)
            M["bc_mu2"] = emit("tensor",
                               lambda t: t.matmul(pbc0[:], onesF[:], rowf[0:1, R_MU:R_MU + TT],
                                                  start=True, stop=True),
                               waits=[M["rstd2"], M["xn"], M["ev_e3"]], inc=sMM)
            M["bc_rs2"] = emit("tensor",
                               lambda t: t.matmul(pbc1[:], onesF[:], rowf[0:1, R_RSTD:R_RSTD + TT],
                                                  start=True, stop=True),
                               waits=[M["rstd2"], M["xn"], M["mul_e1"]], inc=sMM)
            M["xrld"] = emit("sync", lambda s, sp=xsrc: s.dma_start(out=x3, in_=sp),
                             waits=[(sMM, sMM.n), M["tau"]], inc=sX, amt=16)

            out_marks = []
            ydst = y_h.rearrange("(c p) t -> p c t", p=128)[:, :, t0:t0 + TT]
            st_marks = []
            for hw in range(2):
                cs, ce = hw * 4, hw * 4 + 4
                def _tl(v, a=cs, b=ce):
                    for c in range(a, b):
                        v.tensor_tensor(P3[:, c, 0:TT], r3[:, c], pbc0[:], ALU.subtract)
                    for c in range(a, b):
                        ins = v.tensor_tensor(P3[:, c, 0:TT], P3[:, c, 0:TT], pbc1[:], ALU.mult)
                    return ins
                tl = emit("vector", _tl, waits=[M["bc_mu2"], M["bc_rs2"], M["carry"]], inc=sDV)

                def _zm(v, a=cs, b=ce):
                    for c in range(a, b):
                        ins = v.tensor_scalar(Q3[:, c, 0:TT], P3[:, c, 0:TT],
                                              vcol(V_GAM, c), vcol(V_BET, c), ALU.mult, ALU.add)
                    return ins
                zm = emit("vector", _zm, waits=[tl], inc=sDV)
                t2m = emit("scalar",
                           lambda s, a=cs, b=ce: s.activation(B3[:, a:b, :], Q3[:, a:b, 0:TT], AF.Tanh,
                                                              bias=0.0, scale=0.5),
                           waits=[zm, M["ybf"]], inc=sAC)
                e2m = emit(EPOOL,
                           lambda v, a=cs, b=ce: v.tensor_tensor(P3[:, a:b, 0:TT], Q3[:, a:b, 0:TT],
                                                                 B3[:, a:b, :], ALU.mult),
                           waits=[t2m], inc=sEP)
                g2m = emit(EPOOL,
                           lambda v, a=cs, b=ce: v.tensor_tensor(B3[:, a:b, :], Q3[:, a:b, 0:TT],
                                                                 P3[:, a:b, 0:TT], ALU.add),
                           waits=[e2m], inc=sEP)
                t3m = emit("vector",
                           lambda v, a=cs, b=ce: v.tensor_tensor(Q3[:, a:b, 0:TT], v3[:, a:b, :],
                                                                 B3[:, a:b, :], ALU.mult),
                           waits=[g2m, M["oev7"]], inc=sDV)
                om = emit(EPOOL,
                          lambda v, a=cs, b=ce: v.tensor_tensor(P3[:, a:b, 0:TT], x3[:, a:b],
                                                                Q3[:, a:b, 0:TT], ALU.add),
                          waits=[t3m, M["xrld"]], inc=sEP)
                out_marks.append(om)
                M[f"out_w{hw}"] = om
                st_marks.append(emit("sync",
                                     lambda s, a=cs, b=ce, yd=ydst: s.dma_start(out=yd[:, a:b], in_=P3[:, a:b, 0:TT]),
                                     waits=[om], inc=sST, amt=16))
                M["t_ln2"] = tl
                M["z"] = zm
                M["g2"] = g2m
                M["t3"] = t3m
            M["out"] = out_marks[-1]
            M["store"] = st_marks[-1]

            m_prev_out = M["out"]
            m_prev_store = M["store"]
            m_prev_t3 = M["t3"]
            mt = {"ln2_sum1": M["ln2_sum1"], "ln2_sum2": M["ln2_sum2"],
                  "ffn_res": M["ffn_res"], "w1_last": M["w1_last"], "t3": M["t3"],
                  "t_ln2": M["t_ln2"], "wo2_last": M["wo2_last"], "f": M["f"],
                  "masks4": M["masks4"], "mrows": M["mrows"],
                  "maskbc_last": M["bc_e3"], "rowf_guard": M["bc_rs2"],
                  "out_w0": M["out_w0"], "out_w1": M["out_w1"],
                  "pbc_guard": M["t_ln2"], "rowread0": M["var2"], "rowread1": M["var2"]}

        emit("sync", lambda s: s.nop(), waits=[(sST, sST.n)])

        def runner(name):
            cache = {}

            def run(eng):
                for waits, fn, inch, amt in prog[name]:
                    for (s, v) in waits:
                        if v > 0 and cache.get(id(s), 0) < v:
                            eng.wait_ge(s.h, v)
                            cache[id(s)] = v
                    ins = fn(eng)
                    if inch is not None:
                        ins.then_inc(inch, amt)
            return run

        block.sync(runner("sync"))
        block.vector(runner("vector"))
        block.scalar(runner("scalar"))
        block.tensor(runner("tensor"))
        block.gpsimd(runner("gpsimd"))

    return nc


def kernel(**inputs):
    x = np.asarray(inputs["L0_attn_x"], dtype=np.float32)
    xT = np.ascontiguousarray(np.transpose(x, (0, 2, 1)))
    wkT = np.ascontiguousarray(np.asarray(inputs["L0_attn_W_k"], np.float32).T)
    wvT = np.ascontiguousarray(np.asarray(inputs["L0_attn_W_v"], np.float32).T)
    wrT = np.ascontiguousarray(np.asarray(inputs["L0_attn_W_r"], np.float32).T)
    woT = np.ascontiguousarray(np.asarray(inputs["L0_attn_W_o"], np.float32).T)
    w1 = np.ascontiguousarray(
        np.asarray(inputs["L0_ffn_W1"], np.float32)
        .reshape(NE, CH, 128, NIC, 128).transpose(3, 2, 0, 1, 4))
    wo2 = np.ascontiguousarray(
        np.asarray(inputs["L0_ffn_Wo"], np.float32)
        .reshape(NIC, 128, CH, 128).transpose(2, 1, 0, 3))
    b1 = np.ascontiguousarray(np.asarray(inputs["L0_ffn_b1"], np.float32))
    wrt = np.ascontiguousarray(
        np.asarray(inputs["L0_ffn_W_router"], np.float32)
        .reshape(CH, 128, NE).transpose(1, 0, 2).reshape(128, CH * NE))
    vec_list = [inputs["L0_attn_time_mix_k"], inputs["L0_attn_time_mix_v"],
                inputs["L0_attn_time_mix_r"], inputs["L0_attn_time_decay"],
                inputs["L0_attn_time_first"], inputs["L0_ffn_ln_gamma"],
                inputs["L0_ffn_ln_beta"], inputs["L0_ffn_bo"]]
    vecs = np.ascontiguousarray(
        np.stack([np.asarray(v, np.float32) for v in vec_list])
        .reshape(8, CH, 128).transpose(2, 0, 1).reshape(128, 64))

    nc = build()
    shared = {"wkT": wkT, "wvT": wvT, "wrT": wrT, "woT": woT, "w1": w1,
              "wo2": wo2, "b1": b1, "wrt": wrt, "vecs_in": vecs}
    in_maps = [dict(shared, x=np.ascontiguousarray(xT[c])) for c in range(8)]
    res = run_bass_kernel_spmd(nc, in_maps, core_ids=list(range(8)))
    out = np.stack([np.ascontiguousarray(np.asarray(r["y"], np.float32).T) for r in res.results])
    return out.astype(np.float32)


if __name__ == "__main__":
    import reference as _r
    inp = _r.setup_inputs()
    inp = {k: np.asarray(v) for k, v in inp.items()}
    got = kernel(**inp)
    ref = np.asarray(_r.reference(**inp))
    rv = float(np.mean((got - ref) ** 2) / np.mean(ref ** 2))
    print("resid-var ratio:", rv)


# version 19
# speedup vs baseline: 1.0589x; 1.0441x over previous
import numpy as np
import ml_dtypes
import concourse.bass as bass
import concourse.mybir as mybir
from concourse.bass_utils import run_bass_kernel_spmd
from contextlib import ExitStack

F32 = mybir.dt.float32
BF16 = mybir.dt.bfloat16
AF = mybir.ActivationFunctionType
ALU = mybir.AluOpType

D = 1024
T = 2048
CH = 8
TT = 512
NT = T // TT
I_FF = 4096
NE = 4
NIC = I_FF // 128

V_TMK, V_TMV, V_TMR, V_W, V_U, V_GAM, V_BET, V_BO = range(8)
R_MU, R_RSTD, R_LNT, R_M, R_LROW = 0, 512, 1024, 512, 0


class Sem:
    def __init__(self, ctx, nc, name):
        self.h = ctx.enter_context(nc.semaphore(name))
        self.n = 0

    def bump(self, amt):
        self.n += amt
        return (self, self.n)


def build():
    nc = bass.Bass("TRN2", target_bir_lowering=False, dynamic_dma_scratch_size=4096,
                   detect_race_conditions=False)

    x_h = nc.dram_tensor("x", [D, T], F32, kind="ExternalInput").ap()
    wk_h = nc.dram_tensor("wkT", [D, D], F32, kind="ExternalInput").ap()
    wv_h = nc.dram_tensor("wvT", [D, D], F32, kind="ExternalInput").ap()
    wr_h = nc.dram_tensor("wrT", [D, D], F32, kind="ExternalInput").ap()
    wo_h = nc.dram_tensor("woT", [D, D], F32, kind="ExternalInput").ap()
    w1_h = nc.dram_tensor("w1", [NIC, 128, NE, CH, 128], F32, kind="ExternalInput").ap()
    wo2_h = nc.dram_tensor("wo2", [CH, 128, NIC, 128], F32, kind="ExternalInput").ap()
    b1_h = nc.dram_tensor("b1", [NE, I_FF], F32, kind="ExternalInput").ap()
    wrt_h = nc.dram_tensor("wrt", [128, CH * NE], F32, kind="ExternalInput").ap()
    vecs_h = nc.dram_tensor("vecs_in", [128, 64], F32, kind="ExternalInput").ap()
    y_h = nc.dram_tensor("y", [D, T], F32, kind="ExternalOutput").ap()

    ck_h = nc.dram_tensor("ck", [D, D], BF16, kind="Internal").ap()
    cv_h = nc.dram_tensor("cv", [D, D], BF16, kind="Internal").ap()
    cr_h = nc.dram_tensor("cr", [D, D], BF16, kind="Internal").ap()
    co_h = nc.dram_tensor("co", [D, D], BF16, kind="Internal").ap()
    cw1_h = nc.dram_tensor("cw1", [NIC, 128, NE, CH, 128], BF16, kind="Internal").ap()
    cwo2_h = nc.dram_tensor("cwo2", [CH, 128, NIC, 128], BF16, kind="Internal").ap()

    with ExitStack() as ctx:
        def sb(name, shp, dt=F32):
            return ctx.enter_context(nc.sbuf_tensor(name, shp, dt))

        xb = sb("xb", [128, CH * TT])
        xn = sb("xn", [128, CH * TT])
        kb = sb("kb", [128, CH * TT])
        vb = sb("vb", [128, CH * TT])
        rb = sb("rb", [128, CH * TT])
        Ab = sb("Ab", [128, CH * TT])
        Bb = sb("Bb", [128, CH * TT])
        Pb = sb("Pb", [128, CH * (TT + 1)])
        Qb = sb("Qb", [128, CH * (TT + 1)])
        sbf1 = sb("sbf1", [128, CH * TT], BF16)
        sbf2 = sb("sbf2", [128, CH * TT], BF16)
        sbf3 = sb("sbf3", [128, CH * TT], BF16)
        sbf5 = sb("sbf5", [128, CH * TT], BF16)
        ws = [sb("ws0", [128, 4096], BF16), sb("ws1", [128, 4096], BF16)]
        b1b = sb("b1b", [NE, I_FF], BF16)
        wrtb = sb("wrtb", [128, CH * NE], BF16)
        vecs = sb("vecs", [128, 64], F32)
        pc = sb("pc", [128, CH], F32)
        qc = sb("qc", [128, CH], F32)
        xnl = sb("xnl", [128, CH], F32)
        rowf = sb("rowf", [1, 2048], F32)
        masks4 = sb("masks4", [NE, TT], BF16)
        onesK = sb("onesK", [128, 1], BF16)
        onesrB = sb("onesrB", [1, 128], BF16)
        onesF = sb("onesF", [1, 128], F32)
        ones4F = sb("ones4F", [1, NE], F32)
        epsb = sb("epsb", [1, 2], F32)

        pbank = [ctx.enter_context(nc.psum_tensor(f"pb{i}", [128, TT], F32)) for i in range(4)]
        pbc0 = ctx.enter_context(nc.psum_tensor("pbc0", [128, TT], F32))
        pbc1 = ctx.enter_context(nc.psum_tensor("pbc1", [128, TT], F32))
        prow0 = ctx.enter_context(nc.psum_tensor("prow0", [NE, TT], F32))
        prow1 = ctx.enter_context(nc.psum_tensor("prow1", [NE, TT], F32))

        sP = Sem(ctx, nc, "sP")
        sG = [Sem(ctx, nc, "sG0"), Sem(ctx, nc, "sG1")]
        sU = [Sem(ctx, nc, "sU0"), Sem(ctx, nc, "sU1")]
        sS = [Sem(ctx, nc, "sS0"), Sem(ctx, nc, "sS1")]
        sV = Sem(ctx, nc, "sV")
        sX = Sem(ctx, nc, "sX")
        sR = Sem(ctx, nc, "sR")
        sW = Sem(ctx, nc, "sW")
        sMM = Sem(ctx, nc, "sMM")
        sDV = Sem(ctx, nc, "sDV")
        sAC = Sem(ctx, nc, "sAC")
        sPL = Sem(ctx, nc, "sPL")
        sST = Sem(ctx, nc, "sST")

        block = ctx.enter_context(nc.Block())
        prog = {k: [] for k in ("sync", "vector", "scalar", "tensor", "gpsimd")}

        def emit(ename, fn, waits=(), inc=None, amt=1):
            mark = inc.bump(amt) if inc is not None else None
            prog[ename].append((tuple(w for w in waits if w is not None), fn,
                                inc.h if inc else None, amt))
            return mark

        def c3(t):
            return t.ap().rearrange("p (c t) -> p c t", c=CH)

        x3 = c3(xb); xn3 = c3(xn); k3 = c3(kb); v3 = c3(vb); r3 = c3(rb)
        A3 = c3(Ab); B3 = c3(Bb)
        P3 = Pb.ap().rearrange("p (c t) -> p c t", c=CH)
        Q3 = Qb.ap().rearrange("p (c t) -> p c t", c=CH)
        Ppre = P3[:, :, 0:TT]; Pcur = P3[:, :, 1:TT + 1]
        Qpre = Q3[:, :, 0:TT]
        sb13 = c3(sbf1); sb23 = c3(sbf2); sb33 = c3(sbf3)
        kbf = kb.ap().bitcast(BF16)
        Abf = Ab.ap().bitcast(BF16)
        Bbf = Bb.ap().bitcast(BF16)
        xbbf = xb.ap().bitcast(BF16)

        rbbf = rb.ap().bitcast(BF16)
        mixv3 = Bbf[:, 0:CH * TT].rearrange("p (c t) -> p c t", c=CH)
        mixr3 = Bbf[:, CH * TT:2 * CH * TT].rearrange("p (c t) -> p c t", c=CH)
        mrow = sbf5.ap()[0:1, 2048:4096]
        logits4 = Pb.ap()[0:4, 0:TT]
        rbf3 = rbbf[:, 0:CH * TT].rearrange("p (c t) -> p c t", c=CH)

        def xne_ap(e):
            return [sbf3.ap(), sbf2.ap(), sbf5.ap(),
                    rbbf[:, CH * TT:2 * CH * TT]][e]

        def hview(ic):
            if ic < 16:
                return Abf[:, ic * TT:(ic + 1) * TT]
            return xbbf[:, (ic - 16) * TT:(ic - 15) * TT]

        def vcol(v, c):
            return vecs[:, v * CH + c:v * CH + c + 1]

        m_prep = {}
        prev_prep = [None]

        def prep(name, fn):
            mk = emit("gpsimd", fn, waits=[prev_prep[0]], inc=sP, amt=16)
            prev_prep[0] = mk
            m_prep[name] = mk
            return mk
        prep("k", lambda g: g.dma_start(out=ck_h, in_=wk_h))
        prep("v", lambda g: g.dma_start(out=cv_h, in_=wv_h))
        prep("r", lambda g: g.dma_start(out=cr_h, in_=wr_h))
        prep("wrt", lambda g: g.dma_start(out=wrtb[:], in_=wrt_h))
        prep("b1", lambda g: g.dma_start(out=b1b[:], in_=b1_h))
        w1q = list(range(16))
        for j in w1q[:2]:
            prep(("w1q", j), lambda g, j_=j: g.dma_start(out=cw1_h[2 * j_:2 * (j_ + 1)],
                                                         in_=w1_h[2 * j_:2 * (j_ + 1)]))
        prep("o", lambda g: g.dma_start(out=co_h, in_=wo_h))
        for j in w1q[2:]:
            prep(("w1q", j), lambda g, j_=j: g.dma_start(out=cw1_h[2 * j_:2 * (j_ + 1)],
                                                         in_=w1_h[2 * j_:2 * (j_ + 1)]))
        prep(("wo2h", 0), lambda g: g.dma_start(out=cwo2_h[0:4], in_=wo2_h[0:4]))
        prep(("wo2h", 1), lambda g: g.dma_start(out=cwo2_h[4:8], in_=wo2_h[4:8]))
        m_prep_all = m_prep["b1"]
        emit("sync", lambda s: s.dma_start(out=vecs[:], in_=vecs_h), inc=sV, amt=16)
        m_vecs = (sV, sV.n)

        def _init(v):
            v.memset(pc[:], -1e30)
            v.memset(qc[:], 0.0)
            v.memset(xnl[:], 0.0)
            v.memset(onesK[:], 1.0)
            v.memset(onesrB[:], 1.0)
            v.memset(onesF[:], 1.0)
            v.memset(ones4F[:], 1.0)
            v.memset(epsb[0:1, 0:1], 1e-5)
            return v.memset(epsb[0:1, 1:2], 1e-12)
        emit("vector", _init, inc=sDV)
        m_init = (sDV, sDV.n)

        slot_busy = [None, None]
        unit_loaded = {}
        unit_ctr = [0]
        bank_free = [None, None, None, None]

        def load_unit(uid, src_ap, prep_mark, iu=None):
            n = unit_ctr[0]; unit_ctr[0] += 1
            slot = n % 2
            mk = emit("sync", lambda s, sl=slot, sp=src_ap: s.dma_start(out=ws[sl][:], in_=sp),
                      waits=[prep_mark, slot_busy[slot]], inc=sU[slot], amt=16)
            unit_loaded[uid] = (mk, slot)
            return slot

        ckr = {"k": ck_h.rearrange("(dc dp) e -> dp dc e", dp=128),
               "v": cv_h.rearrange("(dc dp) e -> dp dc e", dp=128),
               "r": cr_h.rearrange("(dc dp) e -> dp dc e", dp=128),
               "o": co_h.rearrange("(dc dp) e -> dp dc e", dp=128)}

        m_prev_out = None
        m_prev_store = None
        m_prev_t3 = None
        mt = {}

        for ti in range(NT):
            EPOOL = "vector" if ti == 0 else "gpsimd"
            sEP = sDV if EPOOL == "vector" else sPL
            t0 = ti * TT
            M = {}

            xsrc = x_h.rearrange("(c p) t -> p c t", p=128)[:, :, t0:t0 + TT]
            xld_h = []
            for hw in range(2):
                xld_h.append(emit("sync",
                             lambda s, sp=xsrc, a=hw * 4, b=hw * 4 + 4: s.dma_start(out=x3[:, a:b], in_=sp[:, a:b]),
                             waits=[mt.get(f"out_w{hw}")], inc=sX, amt=16))
            M["xld"] = xld_h[-1]

            xbf_h = []
            xsq_h = []
            for hw in range(2):
                a, bnd = hw * 4, hw * 4 + 4
                xbf_h.append(emit("vector",
                             lambda v, a_=a, b_=bnd: v.tensor_copy(sb13[:, a_:b_].rearrange("p c t -> p (c t)"), x3[:, a_:b_].rearrange("p c t -> p (c t)")),
                             waits=[xld_h[hw], mt.get("ln2_sum1"), m_init], inc=sDV))
                xsq_h.append(emit("scalar",
                             lambda s, a_=a, b_=bnd: s.activation(sb23[:, a_:b_], x3[:, a_:b_], AF.Square),
                             waits=[xld_h[hw], mt.get("ln2_sum2"), mt.get("w1_last")], inc=sAC))
            M["xbf"] = xbf_h[-1]
            M["xsq"] = xsq_h[-1]

            def stats_group(dst, rhs3, hmarks=None):
                def fn(t):
                    for c in range(CH):
                        if hmarks is not None:
                            s_, v_ = hmarks[c // 4]
                            t.wait_ge(s_.h, v_)
                        ins = t.matmul(dst[0:1, :], onesK[:], rhs3[:, c],
                                       start=(c == 0), stop=(c == CH - 1))
                    return ins
                return fn
            M["sum1"] = emit("tensor", stats_group(prow0, sb13, xbf_h),
                             waits=[mt.get("rowread0")], inc=sMM)
            M["sum2"] = emit("tensor", stats_group(prow1, sb23, xsq_h),
                             waits=[mt.get("rowread1")], inc=sMM)

            def rows_a(v):
                v.tensor_scalar(rowf[:, R_MU:R_MU + TT], prow0[0:1, :], 1.0 / D, None, ALU.mult)
                return v.tensor_scalar(rowf[:, R_RSTD:R_RSTD + TT], prow1[0:1, :], 1.0 / D, None, ALU.mult)

            def rows_b(v):
                v.tensor_tensor(rowf[:, R_LNT:R_LNT + TT], rowf[:, R_MU:R_MU + TT],
                                rowf[:, R_MU:R_MU + TT], ALU.mult)
                return v.tensor_tensor(rowf[:, R_RSTD:R_RSTD + TT], rowf[:, R_RSTD:R_RSTD + TT],
                                       rowf[:, R_LNT:R_LNT + TT], ALU.subtract)
            _ra1 = emit("vector", rows_a,
                        waits=[M["sum1"], M["sum2"], mt.get("rowf_guard")], inc=sDV)
            M["var1"] = emit("vector", rows_b, waits=[_ra1], inc=sDV)

            def rstd_a(eps):
                def fn(s):
                    s.activation(rowf[:, R_LNT:R_LNT + TT], rowf[:, R_RSTD:R_RSTD + TT],
                                 AF.Ln, bias=eps, scale=1.0)
                    return s.activation(rowf[:, R_RSTD:R_RSTD + TT], rowf[:, R_LNT:R_LNT + TT],
                                        AF.Exp, bias=0.0, scale=-0.5)
                return fn
            M["rstd1"] = emit("scalar", rstd_a(epsb[0:1, 0:1]), waits=[M["var1"], m_init], inc=sAC)

            M["bc_mu1"] = emit("tensor",
                               lambda t: t.matmul(pbc0[:], onesF[:], rowf[0:1, R_MU:R_MU + TT],
                                                  start=True, stop=True),
                               waits=[M["rstd1"], mt.get("pbc_guard")], inc=sMM)
            M["bc_rs1"] = emit("tensor",
                               lambda t: t.matmul(pbc1[:], onesF[:], rowf[0:1, R_RSTD:R_RSTD + TT],
                                                  start=True, stop=True),
                               waits=[M["rstd1"], mt.get("pbc_guard")], inc=sMM)

            def apply_ln1(v):
                for c in range(CH):
                    v.tensor_tensor(xn3[:, c], x3[:, c], pbc0[:], ALU.subtract)
                for c in range(CH):
                    ins = v.tensor_tensor(xn3[:, c], xn3[:, c], pbc1[:], ALU.mult)
                return ins
            M["xn"] = emit("vector", apply_ln1,
                           waits=[M["bc_mu1"], M["bc_rs1"], mt.get("ffn_res")], inc=sDV)
            M["xnbf"] = emit("vector", lambda v: v.tensor_copy(sbf2[:], xn[:]),
                             waits=[M["xsq"]], inc=sDV)

            def router(t):
                for c in range(CH):
                    ins = t.matmul(prow0[0:NE, :], wrtb[:, c * NE:(c + 1) * NE], sb23[:, c],
                                   start=(c == 0), stop=(c == CH - 1))
                return ins
            M["router"] = emit("tensor", router,
                               waits=[M["xnbf"], m_prep_all, M["var1"]], inc=sMM)

            def diffs(v):
                v.tensor_tensor(A3[:, :, 1:], xn3[:, :, 1:], xn3[:, :, 0:TT - 1], ALU.subtract)
                return v.tensor_tensor(A3[:, :, 0], xn3[:, :, 0], xnl.ap(), ALU.subtract)
            M["diff"] = emit("vector", diffs, waits=[M["xn"], mt.get("wo2_last")], inc=sDV)

            def mk_mix(dst3, vidx):
                def fn(v):
                    for c in range(CH):
                        v.scalar_tensor_tensor(dst3[:, c, 1:], A3[:, c, 1:], vcol(vidx, c),
                                               xn3[:, c, 0:TT - 1], ALU.mult, ALU.add)
                        ins = v.scalar_tensor_tensor(dst3[:, c, 0:1], A3[:, c, 0:1], vcol(vidx, c),
                                                     xnl[:, c:c + 1], ALU.mult, ALU.add)
                    return ins
                return fn
            M["mixk"] = emit("vector", mk_mix(sb33, V_TMK),
                             waits=[M["diff"], m_vecs, mt.get("w1_last")], inc=sDV)
            M["mixv"] = emit("vector", mk_mix(mixv3, V_TMV),
                             waits=[M["diff"], m_vecs, mt.get("t3")], inc=sDV)
            M["mixr"] = emit("vector", mk_mix(mixr3, V_TMR),
                             waits=[M["diff"], m_vecs], inc=sDV)

            IU = {"k": 0, "v": 2, "r": 4, "o": 38}

            def proj_groups(nm, rhs3, mixmark, evac):
                load_unit((ti, nm, 0), ckr[nm][:, :, 0:512], m_prep[nm])
                load_unit((ti, nm, 1), ckr[nm][:, :, 512:1024], m_prep[nm])
                for ec in range(CH):
                    mku, slot = unit_loaded[(ti, nm, 0 if ec < 4 else 1)]
                    bank = pbank[ec % 4]
                    wsl = ws[slot].ap().rearrange("p (dc e) -> p dc e", dc=CH)
                    eo = (ec % 4) * 128

                    def fn(t, w=wsl, b=bank, eo_=eo, r3_=rhs3):
                        for dc in range(CH):
                            ins = t.matmul(b[:], w[:, dc, eo_:eo_ + 128], r3_[:, dc],
                                           start=(dc == 0), stop=(dc == CH - 1))
                        return ins
                    gm = emit("tensor", fn, waits=[mku, mixmark, bank_free[ec % 4]], inc=sMM)
                    M[f"{nm}grp{ec}"] = gm
                    ev = evac(gm, ec, bank)
                    M[f"{nm}ev{ec}"] = ev
                    bank_free[ec % 4] = ev
                    if ec == 3:
                        slot_busy[unit_loaded[(ti, nm, 0)][1]] = gm
                    if ec == 7:
                        slot_busy[unit_loaded[(ti, nm, 1)][1]] = gm

            proj_groups("k", sb33, M["mixk"],
                        lambda gm, ec, bank: emit("scalar",
                            lambda s, b=bank, e=ec: s.activation(k3[:, e], b[:], AF.Copy),
                            waits=[gm, mt.get("f")], inc=sAC))
            M["lg4"] = emit("vector", lambda v: v.tensor_copy(logits4, prow0[0:NE, :]),
                            waits=[M["router"], mt.get("masks4"), m_prev_store], inc=sDV)
            proj_groups("v", mixv3, M["mixv"],
                        lambda gm, ec, bank: emit("vector",
                            lambda v, b=bank, e=ec: v.tensor_copy(v3[:, e], b[:]),
                            waits=[gm, mt.get("t3")], inc=sDV))

            M["realign"] = emit("sync", lambda s: s.dma_start(
                out=rowf.ap()[0:1, R_LROW:R_LROW + NE * TT].rearrange("p (e t) -> p e t", e=NE),
                in_=logits4), waits=[M["lg4"], mt.get("mrows"), M["bc_mu1"], M["bc_rs1"]],
                inc=sR, amt=16)

            def maxes(v):
                L = rowf[0:1, :]
                v.tensor_tensor(L[:, 0:TT], L[:, 0:TT], L[:, TT:2 * TT], ALU.max)
                v.tensor_tensor(L[:, 2 * TT:3 * TT], L[:, 2 * TT:3 * TT], L[:, 3 * TT:4 * TT], ALU.max)
                return v.tensor_tensor(L[:, R_M:R_M + TT], L[:, 0:TT], L[:, 2 * TT:3 * TT], ALU.max)
            M["mrows"] = emit("vector", maxes,
                              waits=[M["realign"], M["rstd1"], M["bc_mu1"], M["bc_rs1"]],
                              inc=sDV)
            proj_groups("r", mixr3, M["mixr"],
                        lambda gm, ec, bank: emit("scalar",
                            lambda s, b=bank, e=ec: s.activation(rbf3[:, e], b[:], AF.Tanh,
                                                                 bias=0.0, scale=0.5),
                            waits=[gm, mt.get("t_ln2")], inc=sAC))
            M["xnl"] = emit("vector", lambda v: v.tensor_copy(xnl[:], xn3[:, :, TT - 1]),
                            waits=[M["mixv"], M["mixr"]], inc=sDV)
            M["rfix"] = emit("vector",
                             lambda v: v.tensor_scalar(rbbf[:, 0:CH * TT], rbbf[:, 0:CH * TT],
                                                       0.5, 0.5, ALU.mult, ALU.add),
                             waits=[M["rev7"]], inc=sDV)

            M["m4bc"] = emit("tensor",
                             lambda t: t.matmul(prow1[0:NE, :], ones4F[:], rowf[0:1, R_M:R_M + TT],
                                                start=True, stop=True),
                             waits=[M["mrows"], M["var1"]], inc=sMM)
            M["masks4"] = emit("vector",
                               lambda v: v.tensor_tensor(masks4[:], logits4, prow1[0:NE, :], ALU.is_equal),
                               waits=[M["m4bc"]], inc=sDV)
            M["mrowre"] = emit("sync", lambda s: s.dma_start(
                out=mrow.rearrange("p (e t) -> p e t", e=NE), in_=masks4[:]),
                waits=[M["masks4"], mt.get("maskbc_last"), mt.get("w1_last")], inc=sR, amt=16)

            def bc_mask(e, dst, extra):
                return emit("tensor",
                            lambda t, e_=e, d=dst: t.matmul(d[:], onesrB[:],
                                                            mrow[:, e_ * TT:(e_ + 1) * TT],
                                                            start=True, stop=True),
                            waits=[M["mrowre"], M["xn"]] + extra, inc=sMM)

            def ev_mask(eng, dstbuf, psum, extra):
                def fn(s, d=dstbuf, ps=psum):
                    d3 = d.rearrange("p (c t) -> p c t", c=CH)
                    for c in range(CH):
                        ins = (s.activation(d3[:, c], ps[:], AF.Copy) if eng == "scalar"
                               else s.tensor_copy(d3[:, c], ps[:]))
                    return ins
                return emit(eng, fn, waits=extra, inc=(sAC if eng == "scalar" else sDV))

            M["bc_e0"] = bc_mask(0, pbc0, [])
            M["bc_e1"] = bc_mask(1, pbc1, [])
            M["ev_e0"] = ev_mask("scalar", xne_ap(0), pbc0, [M["bc_e0"], M["rgrp7"], mt.get("w1_last")])
            M["bc_e3"] = bc_mask(3, pbank[0], [bank_free[0]])
            M["ev_e3"] = ev_mask("vector", xne_ap(3), pbank[0], [M["bc_e3"], mt.get("t_ln2")])
            bank_free[0] = M["ev_e3"]
            M["bc_e2"] = bc_mask(2, pbc0, [M["ev_e0"]])
            M["ev_e2"] = ev_mask("scalar", xne_ap(2), pbc0, [M["bc_e2"], mt.get("w1_last")])

            xne_by_e = {}
            for e, eng in ((0, EPOOL), (2, EPOOL), (3, "vector")):
                def fn(v, e_=e):
                    dst = xne_ap(e_).rearrange("p (c t) -> p c t", c=CH)
                    for c in range(CH):
                        ins = v.tensor_tensor(dst[:, c], dst[:, c], sb23[:, c], ALU.mult)
                    return ins
                xne_by_e[e] = emit(eng, fn, waits=[M[f"ev_e{e}"], M["xnbf"]],
                                   inc=(sDV if eng == "vector" else sEP))
            xne_marks = list(xne_by_e.values())

            def mul_e1(v):
                for c in range(CH):
                    ins = v.tensor_tensor(sb23[:, c], sb23[:, c], pbc1[:], ALU.mult)
                return ins
            M["mul_e1"] = emit("vector", mul_e1,
                               waits=[M["bc_e1"], M["router"]] + xne_marks, inc=sDV)
            xne_by_e[1] = M["mul_e1"]

            M["pcin"] = emit("vector",
                             lambda v: (v.tensor_copy(P3[:, :, 0], pc.ap()),
                                        v.tensor_copy(Q3[:, :, 0], qc.ap()))[-1],
                             waits=[m_prev_store, m_prev_out, M["realign"]], inc=sDV)

            def pscan(v):
                for c in range(CH):
                    wb = vcol(V_W, c).broadcast_to([128, TT])
                    ins = v.tensor_tensor_scan(P3[:, c, 1:], wb, k3[:, c], P3[:, c, 0:1],
                                               ALU.add, ALU.max)
                return ins
            M["pscan"] = emit("vector", pscan, waits=[M["pcin"], M["kev7"], M["realign"]], inc=sDV)

            def asub(v):
                for c in range(CH):
                    ins = v.scalar_tensor_tensor(A3[:, c], Ppre[:, c], vcol(V_W, c), Pcur[:, c],
                                                 ALU.add, ALU.subtract)
                return ins
            M["asub"] = emit("vector", asub, waits=[M["pscan"], M["mixr"]], inc=sDV)
            M["aexp"] = emit("scalar", lambda s: s.activation(xb[:], Ab[:], AF.Exp),
                             waits=[M["asub"]], inc=sAC)
            M["bsub"] = emit("vector",
                             lambda v: v.tensor_tensor(B3[:, :, :], k3[:, :, :], Pcur, ALU.subtract),
                             waits=[M["pscan"], M["rgrp7"]], inc=sDV)
            M["bexp"] = emit("scalar", lambda s: s.activation(Ab[:], Bb[:], AF.Exp),
                             waits=[M["bsub"]], inc=sAC)
            M["bmul"] = emit("vector", lambda v: v.tensor_mul(Ab[:], Ab[:], vb[:]),
                             waits=[M["bexp"], M["vev7"]], inc=sDV)

            def _sigarg(v):
                for c in range(CH):
                    ins = v.scalar_tensor_tensor(B3[:, c], k3[:, c], vcol(V_U, c), Ppre[:, c],
                                                 ALU.add, ALU.subtract)
                return ins
            M["sigarg"] = emit("vector", _sigarg, waits=[M["bexp"]], inc=sDV)

            def qscan(v):
                for c in range(CH):
                    ins = v.tensor_tensor_scan(Q3[:, c, 1:], x3[:, c], A3[:, c], Q3[:, c, 0:1],
                                               ALU.mult, ALU.add)
                return ins
            M["qscan"] = emit("vector", qscan, waits=[M["aexp"], M["bmul"]], inc=sDV)
            M["tau"] = emit("scalar", lambda s: s.activation(xb[:], Bb[:], AF.Tanh, bias=0.0, scale=0.5),
                            waits=[M["sigarg"], M["qscan"]], inc=sAC)
            M["delta"] = emit(EPOOL, lambda v: v.tensor_tensor(B3[:, :, :], Qpre, v3[:, :, :], ALU.subtract),
                              waits=[M["qscan"], M["tau"]], inc=sEP)
            M["e"] = emit("vector", lambda v: v.tensor_mul(kb[:], xb[:], Bb[:]),
                          waits=[M["tau"], M["delta"]], inc=sDV)
            M["f"] = emit(EPOOL, lambda v: v.tensor_sub(Bb[:], Bb[:], kb[:]),
                          waits=[M["e"]], inc=sEP)
            M["wkv"] = emit("vector",
                            lambda v: v.scalar_tensor_tensor(Bb[:], Bb[:], 0.5, vb[:], ALU.mult, ALU.add),
                            waits=[M["f"]], inc=sDV)
            M["ybf"] = emit("vector", lambda v: v.tensor_mul(sbf1[:], Bb[:], rbbf[:, 0:CH * TT]),
                            waits=[M["wkv"], M["rfix"], M["sum1"]], inc=sDV)
            M["carry"] = emit("vector",
                              lambda v: (v.tensor_copy(pc[:], P3[:, :, TT]),
                                         v.tensor_copy(qc[:], Q3[:, :, TT]))[-1],
                              waits=[M["qscan"]], inc=sDV)

            h_ev = []
            w1_last = None
            for ic in range(NIC):
                load_unit((ti, "w1", ic), cw1_h[ic].rearrange("p e dc i -> p (e dc i)"),
                          m_prep[("w1q", ic // 2)])
                mku, slot = unit_loaded[(ti, "w1", ic)]
                bank = pbank[ic % 4]

                def fn(t, sl=slot, b=bank, ic_=ic, em=dict(xne_by_e)):
                    wsl = ws[sl].ap().rearrange("p (e dc i) -> p e dc i", e=NE, dc=CH)
                    first = True
                    for e in range(NE):
                        s_, v_ = em[e]
                        t.wait_ge(s_.h, v_)
                        rhs = xne_ap(e).rearrange("p (c t) -> p c t", c=CH)
                        for dc in range(CH):
                            t.matmul(b[:], wsl[:, e, dc, :], rhs[:, dc], start=first, stop=False)
                            first = False
                    return t.matmul(b[:], b1b[:, ic_ * 128:(ic_ + 1) * 128], masks4[:],
                                    start=False, stop=True)
                gm = emit("tensor", fn,
                          waits=[mku, bank_free[ic % 4], M["masks4"], m_prep_all],
                          inc=sMM)
                w1_last = gm
                slot_busy[slot] = gm
                hfree = M["qscan"] if ic < 16 else M["e"]
                ev = emit("scalar",
                          lambda s, b=bank, ic_=ic: s.activation(hview(ic_), b[:], AF.Gelu),
                          waits=[gm, hfree], inc=sAC)
                h_ev.append(ev)
                bank_free[ic % 4] = ev
            M["w1_last"] = w1_last

            def s_evac(gm, ec, bank):
                return emit("vector",
                            lambda v, b=bank, e=ec: v.tensor_scalar(v3[:, e], b[:], 0.5, None, ALU.mult),
                            waits=[gm, M["delta"], M["wkv"]], inc=sDV)
            proj_groups("o", sb13, M["ybf"], s_evac)

            fy_ev = []
            wo2_last = None
            for oc in range(CH):
                load_unit((ti, "wo2", oc), cwo2_h[oc].rearrange("p ic o -> p (ic o)"),
                          m_prep[("wo2h", oc // 4)])
                mku, slot = unit_loaded[(ti, "wo2", oc)]
                bank = pbank[oc % 4]

                def fn(t, sl=slot, b=bank):
                    wsl = ws[sl].ap().rearrange("p (ic o) -> p ic o", ic=NIC)
                    for ic in range(NIC):
                        ins = t.matmul(b[:], wsl[:, ic, :], hview(ic),
                                       start=(ic == 0), stop=(ic == NIC - 1))
                    return ins
                gm = emit("tensor", fn, waits=[mku, bank_free[oc % 4], h_ev[-1]], inc=sMM)
                wo2_last = gm
                slot_busy[slot] = gm
                ev = emit("scalar",
                          lambda s, b=bank, o=oc: s.activation(r3[:, o], b[:], AF.Copy),
                          waits=[gm, M["ybf"], m_vecs], inc=sAC)
                fy_ev.append(ev)
                bank_free[oc % 4] = ev
            M["wo2_last"] = wo2_last

            res_marks = []
            y2_marks = []
            ysq_marks = []
            for c in range(CH):
                rm = emit("vector",
                          lambda v, c_=c: v.scalar_tensor_tensor(r3[:, c_], r3[:, c_], vcol(V_BO, c_),
                                                                 xn3[:, c_], ALU.add, ALU.add),
                          waits=[fy_ev[c], M["xnl"], m_vecs, M["oev7"]], inc=sDV)
                res_marks.append(rm)
                y2_marks.append(emit("vector",
                                     lambda v, c_=c: v.tensor_copy(sb13[:, c_], r3[:, c_]),
                                     waits=[rm], inc=sDV))
                ysq_marks.append(emit("scalar",
                                      lambda s, c_=c: s.activation(sb23[:, c_], r3[:, c_], AF.Square),
                                      waits=[rm, M["router"], M["w1_last"], M["mul_e1"]], inc=sAC))
            M["ffn_res"] = res_marks[-1]

            M["y2bf"] = y2_marks[-1]
            M["ysq"] = ysq_marks[-1]

            def stats_group_w(dst, rhs3, cmarks):
                def fn(t):
                    for c in range(CH):
                        s_, v_ = cmarks[c]
                        t.wait_ge(s_.h, v_)
                        ins = t.matmul(dst[0:1, :], onesK[:], rhs3[:, c],
                                       start=(c == 0), stop=(c == CH - 1))
                    return ins
                return fn
            M["ln2_sum1"] = emit("tensor", stats_group_w(prow0, sb13, y2_marks),
                                 waits=[M["lg4"]], inc=sMM)
            M["ln2_sum2"] = emit("tensor", stats_group_w(prow1, sb23, ysq_marks),
                                 waits=[M["masks4"]], inc=sMM)
            _ra2 = emit("vector", rows_a,
                        waits=[M["ln2_sum1"], M["ln2_sum2"], M["m4bc"]], inc=sDV)
            M["var2"] = emit("vector", rows_b, waits=[_ra2], inc=sDV)
            M["rstd2"] = emit("scalar", rstd_a(epsb[0:1, 1:2]), waits=[M["var2"]], inc=sAC)
            M["bc_mu2"] = emit("tensor",
                               lambda t: t.matmul(pbc0[:], onesF[:], rowf[0:1, R_MU:R_MU + TT],
                                                  start=True, stop=True),
                               waits=[M["rstd2"], M["xn"], M["ev_e3"]], inc=sMM)
            M["bc_rs2"] = emit("tensor",
                               lambda t: t.matmul(pbc1[:], onesF[:], rowf[0:1, R_RSTD:R_RSTD + TT],
                                                  start=True, stop=True),
                               waits=[M["rstd2"], M["xn"], M["mul_e1"]], inc=sMM)
            M["xrld"] = emit("sync", lambda s, sp=xsrc: s.dma_start(out=x3, in_=sp),
                             waits=[(sMM, sMM.n), M["tau"]], inc=sX, amt=16)

            out_marks = []
            ydst = y_h.rearrange("(c p) t -> p c t", p=128)[:, :, t0:t0 + TT]
            st_marks = []
            tl_m, z_m, t2_m, e2_m, g2_m, t3_m = [], [], [], [], [], []
            for hw in range(2):
                cs, ce = hw * 4, hw * 4 + 4

                def _tl(v, a=cs, b=ce):
                    for c in range(a, b):
                        v.tensor_tensor(P3[:, c, 0:TT], r3[:, c], pbc0[:], ALU.subtract)
                    for c in range(a, b):
                        ins = v.tensor_tensor(P3[:, c, 0:TT], P3[:, c, 0:TT], pbc1[:], ALU.mult)
                    return ins
                tl_m.append(emit("vector", _tl, waits=[M["bc_mu2"], M["bc_rs2"], M["carry"]], inc=sDV))

                def _zm(v, a=cs, b=ce):
                    for c in range(a, b):
                        ins = v.tensor_scalar(Q3[:, c, 0:TT], P3[:, c, 0:TT],
                                              vcol(V_GAM, c), vcol(V_BET, c), ALU.mult, ALU.add)
                    return ins
                z_m.append(emit("vector", _zm, waits=[tl_m[hw]], inc=sDV))
            for hw in range(2):
                cs, ce = hw * 4, hw * 4 + 4
                t2_m.append(emit("scalar",
                            lambda s, a=cs, b=ce: s.activation(B3[:, a:b, :], Q3[:, a:b, 0:TT], AF.Tanh,
                                                               bias=0.0, scale=0.5),
                            waits=[z_m[hw], M["ybf"]], inc=sAC))
            for hw in range(2):
                cs, ce = hw * 4, hw * 4 + 4
                e2_m.append(emit(EPOOL,
                            lambda v, a=cs, b=ce: v.tensor_tensor(P3[:, a:b, 0:TT], Q3[:, a:b, 0:TT],
                                                                  B3[:, a:b, :], ALU.mult),
                            waits=[t2_m[hw]], inc=sEP))
            for hw in range(2):
                cs, ce = hw * 4, hw * 4 + 4
                g2_m.append(emit("vector",
                            lambda v, a=cs, b=ce: v.tensor_tensor(B3[:, a:b, :], Q3[:, a:b, 0:TT],
                                                                  P3[:, a:b, 0:TT], ALU.add),
                            waits=[e2_m[hw]], inc=sDV))
                t3_m.append(emit("vector",
                            lambda v, a=cs, b=ce: v.tensor_tensor(Q3[:, a:b, 0:TT], v3[:, a:b, :],
                                                                  B3[:, a:b, :], ALU.mult),
                            waits=[g2_m[hw], M["oev7"]], inc=sDV))
            for hw in range(2):
                cs, ce = hw * 4, hw * 4 + 4
                om = emit(EPOOL,
                          lambda v, a=cs, b=ce: v.tensor_tensor(P3[:, a:b, 0:TT], x3[:, a:b],
                                                                Q3[:, a:b, 0:TT], ALU.add),
                          waits=[t3_m[hw], M["xrld"]], inc=sEP)
                out_marks.append(om)
                M[f"out_w{hw}"] = om
            for hw in range(2):
                cs, ce = hw * 4, hw * 4 + 4
                st_marks.append(emit("sync",
                                     lambda s, a=cs, b=ce, yd=ydst: s.dma_start(out=yd[:, a:b], in_=P3[:, a:b, 0:TT]),
                                     waits=[out_marks[hw]], inc=sST, amt=16))
            M["t_ln2"] = tl_m[-1]
            M["z"] = z_m[-1]
            M["g2"] = g2_m[-1]
            M["t3"] = t3_m[-1]
            M["out"] = out_marks[-1]
            M["store"] = st_marks[-1]

            m_prev_out = M["out"]
            m_prev_store = M["store"]
            m_prev_t3 = M["t3"]
            mt = {"ln2_sum1": M["ln2_sum1"], "ln2_sum2": M["ln2_sum2"],
                  "ffn_res": M["ffn_res"], "w1_last": M["w1_last"], "t3": M["t3"],
                  "t_ln2": M["t_ln2"], "wo2_last": M["wo2_last"], "f": M["f"],
                  "masks4": M["masks4"], "mrows": M["mrows"],
                  "maskbc_last": M["bc_e3"], "rowf_guard": M["bc_rs2"],
                  "out_w0": M["out_w0"], "out_w1": M["out_w1"],
                  "pbc_guard": M["t_ln2"], "rowread0": M["var2"], "rowread1": M["var2"]}

        emit("sync", lambda s: s.nop(), waits=[(sST, sST.n)])

        def runner(name):
            cache = {}

            def run(eng):
                for waits, fn, inch, amt in prog[name]:
                    for (s, v) in waits:
                        if v > 0 and cache.get(id(s), 0) < v:
                            eng.wait_ge(s.h, v)
                            cache[id(s)] = v
                    ins = fn(eng)
                    if inch is not None:
                        ins.then_inc(inch, amt)
            return run

        block.sync(runner("sync"))
        block.vector(runner("vector"))
        block.scalar(runner("scalar"))
        block.tensor(runner("tensor"))
        block.gpsimd(runner("gpsimd"))

    return nc


def kernel(**inputs):
    x = np.asarray(inputs["L0_attn_x"], dtype=np.float32)
    xT = np.ascontiguousarray(np.transpose(x, (0, 2, 1)))
    wkT = np.ascontiguousarray(np.asarray(inputs["L0_attn_W_k"], np.float32).T)
    wvT = np.ascontiguousarray(np.asarray(inputs["L0_attn_W_v"], np.float32).T)
    wrT = np.ascontiguousarray(np.asarray(inputs["L0_attn_W_r"], np.float32).T)
    woT = np.ascontiguousarray(np.asarray(inputs["L0_attn_W_o"], np.float32).T)
    w1 = np.ascontiguousarray(
        np.asarray(inputs["L0_ffn_W1"], np.float32)
        .reshape(NE, CH, 128, NIC, 128).transpose(3, 2, 0, 1, 4))
    wo2 = np.ascontiguousarray(
        np.asarray(inputs["L0_ffn_Wo"], np.float32)
        .reshape(NIC, 128, CH, 128).transpose(2, 1, 0, 3))
    b1 = np.ascontiguousarray(np.asarray(inputs["L0_ffn_b1"], np.float32))
    wrt = np.ascontiguousarray(
        np.asarray(inputs["L0_ffn_W_router"], np.float32)
        .reshape(CH, 128, NE).transpose(1, 0, 2).reshape(128, CH * NE))
    vec_list = [inputs["L0_attn_time_mix_k"], inputs["L0_attn_time_mix_v"],
                inputs["L0_attn_time_mix_r"], inputs["L0_attn_time_decay"],
                inputs["L0_attn_time_first"], inputs["L0_ffn_ln_gamma"],
                inputs["L0_ffn_ln_beta"], inputs["L0_ffn_bo"]]
    vecs = np.ascontiguousarray(
        np.stack([np.asarray(v, np.float32) for v in vec_list])
        .reshape(8, CH, 128).transpose(2, 0, 1).reshape(128, 64))

    nc = build()
    shared = {"wkT": wkT, "wvT": wvT, "wrT": wrT, "woT": woT, "w1": w1,
              "wo2": wo2, "b1": b1, "wrt": wrt, "vecs_in": vecs}
    in_maps = [dict(shared, x=np.ascontiguousarray(xT[c])) for c in range(8)]
    res = run_bass_kernel_spmd(nc, in_maps, core_ids=list(range(8)))
    out = np.stack([np.ascontiguousarray(np.asarray(r["y"], np.float32).T) for r in res.results])
    return out.astype(np.float32)


if __name__ == "__main__":
    import reference as _r
    inp = _r.setup_inputs()
    inp = {k: np.asarray(v) for k, v in inp.items()}
    got = kernel(**inp)
    ref = np.asarray(_r.reference(**inp))
    rv = float(np.mean((got - ref) ** 2) / np.mean(ref ** 2))
    print("resid-var ratio:", rv)
